# version 12
# baseline (speedup 1.0000x reference)
import math
import numpy as np
from contextlib import ExitStack
import concourse.bass as bass
import concourse.mybir as mybir
from concourse.bass_utils import run_bass_kernel_spmd

F32 = mybir.dt.float32
BF16 = mybir.dt.bfloat16
AF = mybir.ActivationFunctionType
ALU = mybir.AluOpType
AX = mybir.AxisListType

ENGINES = ("sync", "gpsimd", "vector", "scalar", "tensor")
ESEM = {"sync": "dq", "gpsimd": "sg", "vector": "sv", "scalar": "sa", "tensor": "sp"}

EPS = 1e-5
QEPS = 1e-5
P23 = float(1.5 * 2.0 ** 23)
P52 = float(2.0 ** 52)
P52M = float(1.5 * 2.0 ** 52)


class Rec:

    def __init__(self):
        self.stream = {e: [] for e in ENGINES}
        self.count = {}

    def val(self, sem):
        return self.count.get(sem, 0)

    def emit(self, eng, method=None, kwargs=None, waits=(), inc=None):
        incs = []
        if method is not None:
            if inc is None:
                if method == "dma_start":
                    sem = "dg" if eng == "gpsimd" else "dq"
                    incs = [(sem, 16)]
                else:
                    incs = [(ESEM[eng], 1)]
            elif inc is False:
                incs = []
            else:
                incs = [inc]
        for s, a in incs:
            self.count[s] = self.count.get(s, 0) + a
        self.stream[eng].append((tuple(waits), method, kwargs or {}, incs))
        return self.val(incs[0][0]) if incs else None

    def replay(self, eng_name, eng, sems):
        for waits, method, kwargs, incs in self.stream[eng_name]:
            for s, v in waits:
                if v > 0:
                    eng.wait_ge(sems[s], v)
            if method is None:
                continue
            inst = getattr(eng, method)(**kwargs)
            for s, a in incs:
                inst = inst.then_inc(sems[s], a)


def build_program(D=2048, HEADS=16, T=2048, I=8192, G=512):
    TP = T
    HD = 128
    C = 128
    FC = D // 128
    NT = T // 128
    NP = TP // 128
    IC = I // 128
    NSUP = T // G
    GT = G // 128
    SCALE = float(HD) ** -0.5
    FWC = float(I) ** -0.5
    gammas = [float(g) for g in
              (1.0 - np.exp(np.linspace(math.log(1 / 32), math.log(1 / 512), HEADS)))]

    nc = bass.Bass("TRN2", target_bir_lowering=False)

    x_own = nc.dram_tensor("x_own", [T, D], F32, kind="ExternalInput").ap()
    x_pref = nc.dram_tensor("x_pref", [TP, D], F32, kind="ExternalInput").ap()
    maskin = nc.dram_tensor("maskin", [1, 1], F32, kind="ExternalInput").ap()
    w_attn = {}
    for nm in ("wq", "wk", "wv", "wg", "wo"):
        w_attn[nm] = nc.dram_tensor(nm + "_bc", [FC, D, 128], F32, kind="ExternalInput").ap()
    wgate_bc = nc.dram_tensor("wgate_bc", [IC, D, 128], F32, kind="ExternalInput").ap()
    wup_bc = nc.dram_tensor("wup_bc", [IC, D, 128], F32, kind="ExternalInput").ap()
    wdown_bc = nc.dram_tensor("wdown_bc", [FC, I, 128], F32, kind="ExternalInput").ap()
    ln_w = nc.dram_tensor("ln_w", [1, D], F32, kind="ExternalInput").ap()
    ln_b = nc.dram_tensor("ln_b", [1, D], F32, kind="ExternalInput").ap()
    out = nc.dram_tensor("out", [T, D], F32, kind="ExternalOutput").ap()
    x2_s = nc.dram_tensor("x2_s", [T, D], F32, kind="Internal").ap()
    yln_s = nc.dram_tensor("yln_s", [T, D], BF16, kind="Internal").ap()
    terng_s = nc.dram_tensor("terng_s", [IC, D, 128], BF16, kind="Internal").ap()
    ternu_s = nc.dram_tensor("ternu_s", [IC, D, 128], BF16, kind="Internal").ap()
    ternd_s = nc.dram_tensor("ternd_s", [FC, I, 128], BF16, kind="Internal").ap()
    dfm_s = nc.dram_tensor("dfm_s", [FC, GT, 128, 128], BF16, kind="Internal").ap()
    bounce = nc.dram_tensor("bounce", [1, 1024], F32, kind="Internal").ap()

    es = ExitStack()

    def sbt(name, shape, dt=F32):
        return es.enter_context(nc.sbuf_tensor(name, shape, dt))

    r = Rec()

    def barrier():
        snap = {s: r.val(s) for s in ("dq", "dg", "dg2", "dq2", "dqa", "dqb", "dq3", "dq4", "sv", "sa", "sg", "sp")}
        for e in ENGINES:
            w = [(s, v) for s, v in snap.items() if v > 0]
            r.emit(e, waits=w)

    def wall(*extra):
        base = [(s, r.val(s)) for s in ("sv", "sa", "sp", "sg")]
        return base + list(extra)

    with es:
        A1 = sbt("A1", [128, FC * T], BF16)
        A2 = sbt("A2", [128, FC * T], BF16)
        xt = sbt("xt", [128, D])
        ta = sbt("ta", [128, D])
        Sst = sbt("Sst", [128, HEADS * 128])
        qh = sbt("qh", [128, T], BF16)
        kh = sbt("kh", [128, T], BF16)
        vh = sbt("vh", [128, T], BF16)
        wsb = sbt("wsb", [128, 8192], BF16)
        fbuf = sbt("fbuf", [128, 4 * G])
        sb512 = sbt("sb512", [128, 2 * G], BF16)
        cols = sbt("cols", [128, 640])
        consts = sbt("consts", [128, 256], BF16)
        onesr = sbt("onesr", [1, 128])
        onesb = sbt("onesb", [1, 128], BF16)
        rowbuf = sbt("rowbuf", [1, 2048])
        rowsb = sbt("rowsb", [1, 512], BF16)
        mh = sbt("mh", [128, 128])
        rh = sbt("rh", [128, 128])
        sbf16 = sbt("sbf16", [128, 128], BF16)
        colb = sbt("colb", [128, 4], BF16)
        ktmb = sbt("ktmb", [128, 256], BF16)
        vtmb = sbt("vtmb", [128, 256], BF16)
        qcb = sbt("qcb", [128, 256], BF16)
        ssb = sbt("ssb", [128, 256], BF16)
        hf = sbt("hf", [128, 128])
        it32 = sbt("it32", [128, 1], mybir.dt.int32)
        PA = es.enter_context(nc.psum_tensor("PA", [128, 2048], F32))
        PB = es.enter_context(nc.psum_tensor("PB", [128, 1024], F32))
        PT = es.enter_context(nc.psum_tensor("PT", [128, 1024], BF16))
        PS = es.enter_context(nc.psum_tensor("PS", [128, 512], F32))

        semnames = ("dq", "dg", "dg2", "dq2", "dqa", "dqb", "dq3", "dq4", "dst", "sv", "sa", "sg", "sp")
        sems = {s: es.enter_context(nc.semaphore(s)) for s in semnames}
        block = es.enter_context(nc.Block())

        CO_P, CO_SEL, CO_A, CO_MASK, CO_GN, CO_TMP, CO_MEAN, CO_SQ, CO_RSTD = range(9)
        CO_KD = 16
        CO_KDP = 32
        CO_WS = 288
        CO_MW = 292
        CO_SC = 300
        CO_OS = 316
        CO_SCD = 332
        CO_OSD = 340
        CO_ACC = 352

        def c_(i):
            return cols[:, i:i + 1]

        def row_from_col(col_f32, dst_off):
            vb = r.emit("vector", "tensor_copy", dict(out=colb[:, 0:1], in_=col_f32),
                        waits=wall())
            tb = r.emit("tensor", "transpose",
                        dict(out=PT[0:1, 896:1024], in_=colb[:, 0:1], identity=consts[:, 0:128]),
                        waits=[("sv", vb), ("sa", r.val("sa"))])
            sa1 = r.emit("scalar", "copy",
                         dict(out=rowsb[0:1, dst_off:dst_off + 128], in_=PT[0:1, 896:1024]),
                         waits=[("sp", tb)])
            return sa1

        def bcast_row(row_off, dst_f32, n=128):
            tb = r.emit("tensor", "matmul",
                        dict(out=PS[:, 384:384 + n], lhsT=onesb[:, :],
                             rhs=rowsb[0:1, row_off:row_off + n], start=True, stop=True),
                        waits=[("sa", r.val("sa")), ("sv", r.val("sv"))])
            return r.emit("vector", "tensor_copy", dict(out=dst_f32, in_=PS[:, 384:384 + n]),
                          waits=[("sp", tb)])

        def ln_stats(src_ap, n_feat, scratch=None):
            scr = ta[:, 0:n_feat] if scratch is None else scratch
            a1 = r.emit("scalar", "activation",
                        dict(out=scr, in_=src_ap, func=AF.Square,
                             accum_out=c_(CO_SQ)),
                        waits=[("sv", r.val("sv")), ("sa", r.val("sa")),
                               ("dq", r.val("dq"))])
            v1 = r.emit("vector", "tensor_reduce",
                        dict(out=c_(CO_MEAN), in_=src_ap, axis=AX.X, op=ALU.add),
                        waits=[("sv", r.val("sv"))])
            v1 = r.emit("vector", "tensor_scalar",
                        dict(out=c_(CO_MEAN), in0=c_(CO_MEAN), scalar1=float(1.0 / n_feat),
                             scalar2=None, op0=ALU.mult), waits=[("sv", v1)])
            v2 = r.emit("vector", "tensor_tensor",
                        dict(out=c_(CO_TMP), in0=c_(CO_MEAN), in1=c_(CO_MEAN), op=ALU.mult),
                        waits=[("sv", v1)])
            v2 = r.emit("vector", "scalar_tensor_tensor",
                        dict(out=c_(CO_TMP), in0=c_(CO_SQ), scalar=float(1.0 / n_feat),
                             in1=c_(CO_TMP), op0=ALU.mult, op1=ALU.subtract),
                        waits=[("sv", v2), ("sa", a1)])
            v2 = r.emit("vector", "tensor_scalar",
                        dict(out=c_(CO_TMP), in0=c_(CO_TMP), scalar1=float(EPS), scalar2=None,
                             op0=ALU.add), waits=[("sv", v2)])
            a2 = r.emit("scalar", "activation",
                        dict(out=c_(CO_RSTD), in_=c_(CO_TMP), func=AF.Sqrt),
                        waits=[("sv", v2)])
            v3 = r.emit("vector", "reciprocal", dict(out=c_(CO_RSTD), in_=c_(CO_RSTD)),
                        waits=[("sa", a2)])
            return v3

        def transpose_tile(dst_view, src_tile, first_waits):
            for g0 in range(0, FC, 8):
                gn = min(8, FC - g0)
                last = None
                for j in range(gn):
                    w = (list(first_waits) + [("sa", r.val("sa"))]) if (g0 == 0 and j == 0) \
                        else ([("sa", r.val("sa"))] if j == 0 else [])
                    last = r.emit("tensor", "transpose",
                                  dict(out=PT[:, j * 128:(j + 1) * 128],
                                       in_=src_tile[:, (g0 + j) * 128:(g0 + j + 1) * 128],
                                       identity=consts[:, 0:128]),
                                  waits=w, inc=(("sp", 1) if j == gn - 1 else False))
                for j in range(gn):
                    r.emit("scalar", "copy",
                           dict(out=dst_view(g0 + j), in_=PT[:, j * 128:(j + 1) * 128]),
                           waits=([("sp", last)] if j == 0 else []))
            return r.val("sa")

        r.emit("vector", "memset", dict(ap=onesr[:, :], constant=1.0))
        v0 = r.emit("vector", "memset", dict(ap=onesb[:, :], constant=1.0))
        r.emit("gpsimd", "memset", dict(ap=consts[:, 0:128], constant=0.0))
        gi = r.emit("gpsimd", "affine_select",
                    dict(out=consts[:, 0:128], in_=consts[:, 0:128],
                         compare_op=ALU.not_equal, fill=1.0, base=0,
                         pattern=[[-1, 128]], channel_multiplier=1),
                    waits=[("sg", r.val("sg"))])
        gio = r.emit("gpsimd", "iota",
                     dict(out=it32[:, :], pattern=[[0, 1]], base=0, channel_multiplier=1))
        vP = r.emit("vector", "tensor_copy", dict(out=c_(CO_P), in_=it32[:, :]),
                    waits=[("sg", gio)])
        v1 = r.emit("vector", "memset", dict(ap=c_(CO_SEL), constant=1.0), waits=[("sv", vP)])
        v1 = r.emit("vector", "tensor_scalar",
                    dict(out=c_(CO_SEL), in0=c_(CO_SEL), scalar1=P52, scalar2=None, op0=ALU.add),
                    waits=[("sv", v1)])
        v1 = r.emit("vector", "tensor_scalar",
                    dict(out=c_(CO_SEL), in0=c_(CO_SEL), scalar1=P52, scalar2=None,
                         op0=ALU.subtract), waits=[("sv", v1)])
        vA = r.emit("vector", "tensor_scalar",
                    dict(out=c_(CO_A), in0=c_(CO_SEL), scalar1=P52M, scalar2=P23,
                         op0=ALU.mult, op1=ALU.add), waits=[("sv", v1)])
        dmask = r.emit("sync", "dma_start", dict(out=rowbuf[0:1, 0:1], in_=maskin))
        tmask = r.emit("tensor", "matmul",
                       dict(out=PS[:, 0:1], lhsT=onesr[:, :], rhs=rowbuf[0:1, 0:1],
                            start=True, stop=True),
                       waits=[("dq", dmask), ("sv", r.val("sv"))])
        vM = r.emit("vector", "tensor_copy", dict(out=c_(CO_MASK), in_=PS[:, 0:1]),
                    waits=[("sp", tmask)])
        sa_kd = []
        for h in range(HEADS):
            lg = math.log(gammas[h])
            vpre = r.emit("vector", "tensor_scalar",
                          dict(out=c_(CO_KD + h), in0=c_(CO_P), scalar1=float(-lg),
                               scalar2=float(C * lg), op0=ALU.mult, op1=ALU.add),
                          waits=[("sv", vP)])
            a = r.emit("scalar", "activation",
                       dict(out=c_(CO_KD + h), in_=c_(CO_KD + h), func=AF.Exp),
                       waits=[("sv", vpre)])
            sa_kd.append(a)
        for h in range(HEADS):
            for cc in range(NP):
                age = gammas[h] ** ((NP - 1 - cc) * C)
                r.emit("vector", "tensor_scalar",
                       dict(out=c_(CO_KDP + h * NP + cc), in0=c_(CO_KD + h),
                            scalar1=float(age), scalar2=None, op0=ALU.mult),
                       waits=[("sa", sa_kd[h])])
        vH = r.emit("vector", "memset", dict(ap=hf[:, :], constant=1.0),
                    waits=[("sv", r.val("sv"))])
        for k in range(7):
            v2 = r.emit("vector", "tensor_scalar",
                        dict(out=c_(CO_TMP), in0=c_(CO_P), scalar1=float(2.0 ** -k),
                             scalar2=float(-0.5 + 2.0 ** -(k + 1)), op0=ALU.mult, op1=ALU.add),
                        waits=[("sv", vA), ("sv", vH)])
            v2 = r.emit("vector", "tensor_scalar",
                        dict(out=c_(CO_TMP), in0=c_(CO_TMP), scalar1=c_(CO_A), scalar2=None,
                             op0=ALU.add), waits=[("sv", v2)])
            v2 = r.emit("vector", "tensor_scalar",
                        dict(out=c_(CO_TMP), in0=c_(CO_TMP), scalar1=c_(CO_A), scalar2=None,
                             op0=ALU.subtract), waits=[("sv", v2)])
            v3 = r.emit("vector", "tensor_scalar",
                        dict(out=c_(CO_GN), in0=c_(CO_TMP), scalar1=0.5, scalar2=-0.25,
                             op0=ALU.mult, op1=ALU.add), waits=[("sv", v2)])
            v3 = r.emit("vector", "tensor_scalar",
                        dict(out=c_(CO_GN), in0=c_(CO_GN), scalar1=c_(CO_A), scalar2=None,
                             op0=ALU.add), waits=[("sv", v3)])
            v3 = r.emit("vector", "tensor_scalar",
                        dict(out=c_(CO_GN), in0=c_(CO_GN), scalar1=c_(CO_A), scalar2=None,
                             op0=ALU.subtract), waits=[("sv", v3)])
            v4 = r.emit("vector", "scalar_tensor_tensor",
                        dict(out=c_(CO_GN), in0=c_(CO_GN), scalar=-2.0, in1=c_(CO_TMP),
                             op0=ALU.mult, op1=ALU.add), waits=[("sv", v3)])
            sa1 = row_from_col(c_(CO_GN), 0)
            to = r.emit("tensor", "matmul",
                        dict(out=PS[:, 0:128], lhsT=rowsb[0:1, 0:128], rhs=rowsb[0:1, 0:128],
                             start=True, stop=True),
                        waits=[("sa", sa1), ("sv", r.val("sv"))])
            v5 = r.emit("vector", "tensor_scalar",
                        dict(out=ta[:, 0:128], in0=PS[:, 0:128], scalar1=-2.0, scalar2=1.0,
                             op0=ALU.mult, op1=ALU.add), waits=[("sp", to)])
            vH = r.emit("vector", "tensor_tensor",
                        dict(out=hf[:, :], in0=hf[:, :], in1=ta[:, 0:128], op=ALU.mult),
                        waits=[("sv", v5)])
        vHp = r.emit("vector", "tensor_copy", dict(out=consts[:, 128:256], in_=hf[:, :]),
                     waits=[("sv", vH)])
        lblks = [(i, min(512, D - i)) for i in range(0, D, 512)]
        dln = r.emit("sync", "dma_start", dict(out=rowbuf[0:1, 0:D], in_=ln_w),
                     waits=[("sp", tmask)])
        for bi, (b0, bw) in enumerate(lblks):
            r.emit("tensor", "matmul",
                   dict(out=PA[:, b0:b0 + bw], lhsT=onesr[:, :],
                        rhs=rowbuf[0:1, b0:b0 + bw], start=True, stop=True),
                   waits=([("dq", dln), ("sv", r.val("sv"))] if bi == 0 else []),
                   inc=(("sp", 1) if bi == len(lblks) - 1 else False))
        va = r.emit("vector", "tensor_copy", dict(out=wsb[:, 4096:4096 + D], in_=PA[:, 0:D]),
                    waits=[("sp", r.val("sp"))])
        dlb = r.emit("sync", "dma_start", dict(out=rowbuf[0:1, 0:D], in_=ln_b),
                     waits=[("sv", va)])
        for bi, (b0, bw) in enumerate(lblks):
            r.emit("tensor", "matmul",
                   dict(out=PA[:, b0:b0 + bw], lhsT=onesr[:, :],
                        rhs=rowbuf[0:1, b0:b0 + bw], start=True, stop=True),
                   waits=([("dq", dlb), ("sv", va)] if bi == 0 else []),
                   inc=(("sp", 1) if bi == len(lblks) - 1 else False))
        r.emit("vector", "tensor_copy", dict(out=wsb[:, 4096 + D:4096 + 2 * D], in_=PA[:, 0:D]),
               waits=[("sp", r.val("sp"))])
        barrier()

        def ln1_phase(src_dram, TT):
            ntile = TT // 128
            for t in range(ntile):
                d1 = r.emit("sync", "dma_start",
                            dict(out=xt[:, :], in_=src_dram[t * 128:(t + 1) * 128, :]),
                            waits=wall(("dq", r.val("dq"))))
                r.emit("vector", waits=[("dq", d1)])
                r.emit("scalar", waits=[("dq", d1)])
                vS = ln_stats(xt[:, :], D)
                vap = r.emit("vector", "tensor_scalar",
                             dict(out=wsb[:, 0:D], in0=xt[:, :], scalar1=c_(CO_MEAN),
                                  scalar2=c_(CO_RSTD), op0=ALU.subtract, op1=ALU.mult),
                             waits=[("sv", vS)])
                transpose_tile(
                    lambda k, _t=t: A1[:, k * TT + _t * 128: k * TT + (_t + 1) * 128],
                    wsb[:, 0:D], [("sv", vap)])
            barrier()

        ln1_phase(x_pref, TP)

        def load_wcol(dst_slice, src_bc, idx, waits=()):
            return r.emit("gpsimd", "dma_start",
                          dict(out=dst_slice.rearrange("p (k c) -> p k c", c=128),
                               in_=src_bc[idx].rearrange("(k p) c -> p k c", p=128)),
                          waits=waits)

        def proj_head(wslot, TT, dg_wait, dsem="dg"):
            blks = [(i, min(512, TT - i)) for i in range(0, TT, 512)]
            for bi, (b0, bw) in enumerate(blks):
                for k in range(FC):
                    r.emit("tensor", "matmul",
                           dict(out=PA[:, b0:b0 + bw],
                                lhsT=wsb[:, wslot * D + k * 128: wslot * D + (k + 1) * 128],
                                rhs=A1[:, k * TT + b0: k * TT + b0 + bw],
                                start=(k == 0), stop=(k == FC - 1)),
                           waits=([(dsem, dg_wait), ("sa", r.val("sa")), ("sv", r.val("sv"))]
                                  if (bi == 0 and k == 0) else []),
                           inc=(("sp", 1) if (k == FC - 1 and bi == len(blks) - 1) else False))
            return r.val("sp")

        def proj_head_v(wslot, TT, dg_wait):
            return proj_head(wslot, TT, dg_wait, dsem="dg2")

        sp_projp = {}
        for h in range(HEADS):
            wkd = r.emit("gpsimd", "dma_start",
                         dict(out=wsb[:, 0:D].rearrange("p (k c) -> p k c", c=128),
                              in_=w_attn["wk"][h].rearrange("(k p) c -> p k c", p=128)),
                         waits=[("sp", sp_projp.get(h - 1, 0))], inc=("dg", 16))
            wvd = r.emit("gpsimd", "dma_start",
                         dict(out=wsb[:, D:2 * D].rearrange("p (k c) -> p k c", c=128),
                              in_=w_attn["wv"][h].rearrange("(k p) c -> p k c", p=128)),
                         inc=("dg2", 16))
            spk = proj_head(0, TP, r.val("dg"))
            sak = r.emit("scalar", "copy", dict(out=kh[:, 0:TP], in_=PA[:, 0:TP]),
                         waits=[("sp", spk)])
            spv = proj_head_v(1, TP, r.val("dg2"))
            sp_projp[h] = spv
            sav = r.emit("scalar", "copy", dict(out=vh[:, 0:TP], in_=PA[:, 0:TP]),
                         waits=[("sp", spv)])
            for cc in range(NP):
                cs = slice(cc * 128, (cc + 1) * 128)
                tk1 = r.emit("tensor", "transpose",
                             dict(out=PT[:, 0:128], in_=kh[:, cs], identity=consts[:, 0:128]),
                             waits=[("sa", sav), ("sv", r.val("sv"))])
                tk2 = r.emit("tensor", "transpose",
                             dict(out=PT[:, 128:256], in_=vh[:, cs], identity=consts[:, 0:128]))
                vk = r.emit("vector", "tensor_scalar",
                            dict(out=ktmb[:, 0:128], in0=PT[:, 0:128],
                                 scalar1=c_(CO_KDP + h * NP + cc), scalar2=None, op0=ALU.mult),
                            waits=[("sp", tk2)])
                vv = r.emit("vector", "tensor_copy", dict(out=vtmb[:, 0:128], in_=PT[:, 128:256]),
                            waits=[("sv", vk)])
                r.emit("tensor", "matmul",
                       dict(out=PS[:, 0:128], lhsT=ktmb[:, 0:128], rhs=vtmb[:, 0:128],
                            start=(cc == 0), stop=(cc == NP - 1), skip_group_check=True),
                       waits=[("sv", vv)],
                       inc=(("sp", 1) if cc == NP - 1 else False))
            spS = r.val("sp")
            r.emit("vector", "tensor_scalar",
                   dict(out=Sst[:, h * 128:(h + 1) * 128], in0=PS[:, 0:128],
                        scalar1=c_(CO_MASK), scalar2=None, op0=ALU.mult),
                   waits=[("sp", spS), ("sv", vM)])
        barrier()

        ln1_phase(x_own, T)

        sp_mix = {}
        sp_v = {}
        for h in range(HEADS):
            lg = math.log(gammas[h])
            vpre = r.emit("vector", "tensor_scalar",
                          dict(out=c_(CO_GN), in0=c_(CO_P), scalar1=float(lg),
                               scalar2=float(math.log(SCALE)), op0=ALU.mult, op1=ALU.add),
                          waits=wall())
            aR = r.emit("scalar", "activation",
                        dict(out=c_(CO_GN), in_=c_(CO_GN), func=AF.Exp),
                        waits=[("sv", vpre)])
            saR = row_from_col(c_(CO_GN), 0)
            bcast_row(0, rh[:, :])
            vpre = r.emit("vector", "tensor_scalar",
                          dict(out=c_(CO_GN), in0=c_(CO_P), scalar1=float(-lg),
                               scalar2=None, op0=ALU.mult),
                          waits=[("sa", saR), ("sv", r.val("sv"))])
            aG = r.emit("scalar", "activation",
                        dict(out=c_(CO_GN), in_=c_(CO_GN), func=AF.Exp),
                        waits=[("sv", vpre)])
            saG = row_from_col(c_(CO_GN), 128)
            tM = r.emit("tensor", "matmul",
                        dict(out=PS[:, 0:128], lhsT=rowsb[0:1, 128:256],
                             rhs=rowsb[0:1, 0:128], start=True, stop=True),
                        waits=[("sa", saG), ("sv", r.val("sv"))])
            vMh = r.emit("vector", "tensor_copy", dict(out=mh[:, :], in_=PS[:, 0:128]),
                         waits=[("sp", tM)])
            gMh = r.emit("gpsimd", "affine_select",
                         dict(out=mh[:, :], in_=mh[:, :], compare_op=ALU.is_ge, fill=0.0,
                              base=0, pattern=[[1, 128]], channel_multiplier=-1),
                         waits=[("sv", vMh), ("sg", r.val("sg"))])
            r.emit("vector", waits=[("sg", gMh)])
            sa0 = r.emit("scalar", "copy",
                         dict(out=sbf16[:, :], in_=Sst[:, h * 128:(h + 1) * 128]),
                         waits=[("sv", r.val("sv"))])
            wqd = r.emit("gpsimd", "dma_start",
                         dict(out=wsb[:, 0:D].rearrange("p (k c) -> p k c", c=128),
                              in_=w_attn["wq"][h].rearrange("(k p) c -> p k c", p=128)),
                         waits=[("sp", sp_v.get(h - 1, 0))], inc=("dg", 16))
            wkd = r.emit("gpsimd", "dma_start",
                         dict(out=wsb[:, D:2 * D].rearrange("p (k c) -> p k c", c=128),
                              in_=w_attn["wk"][h].rearrange("(k p) c -> p k c", p=128)),
                         inc=("dg2", 16))
            spq = proj_head(0, T, r.val("dg"))
            saq = r.emit("scalar", "copy", dict(out=qh[:, :], in_=PA[:, 0:T]),
                         waits=[("sp", spq)])
            wvd = r.emit("gpsimd", "dma_start",
                         dict(out=wsb[:, 0:D].rearrange("p (k c) -> p k c", c=128),
                              in_=w_attn["wv"][h].rearrange("(k p) c -> p k c", p=128)),
                         waits=[("sp", spq)], inc=("dg", 16))
            spk = proj_head_v(1, T, r.val("dg2"))
            sak = r.emit("scalar", "copy", dict(out=kh[:, :], in_=PA[:, 0:T]),
                         waits=[("sp", spk)])
            spv = proj_head(0, T, r.val("dg"))
            sp_v[h] = spv
            sav = r.emit("scalar", "copy", dict(out=vh[:, :], in_=PA[:, 0:T]),
                         waits=[("sp", spv)])
            t4_of, t5_of, t6_of = {}, {}, {}
            say_of, sa0_of, vS1_of, vk_of, vS2_of = {}, {-1: sa0}, {}, {}, {}
            for cc in range(NT):
                cs = slice(cc * 128, (cc + 1) * 128)
                pr = cc % 2
                SC_PS = PS[:, 0:128] if pr == 0 else PA[:, 0:128]
                O_PS = PB[:, 0:128] if pr == 0 else PA[:, 512:640]
                DS_PS = PB[:, 512:640] if pr == 0 else PA[:, 1024:1152]
                PTa = PT[:, 0:128] if pr == 0 else PT[:, 256:384]
                PTb = PT[:, 128:256] if pr == 0 else PT[:, 384:512]
                kt = ktmb[:, pr * 128:(pr + 1) * 128]
                vt = vtmb[:, pr * 128:(pr + 1) * 128]
                sb_ = ssb[:, pr * 128:(pr + 1) * 128]
                qc = qcb[:, pr * 128:(pr + 1) * 128]
                t1 = r.emit("tensor", "matmul",
                            dict(out=SC_PS, lhsT=kh[:, cs], rhs=qh[:, cs],
                                 start=True, stop=True, skip_group_check=True),
                            waits=[("sa", sav), ("sv", vS1_of.get(cc - 2, 0))])
                t2 = r.emit("tensor", "transpose",
                            dict(out=PTa, in_=kh[:, cs], identity=consts[:, 0:128]),
                            waits=[("sv", vk_of.get(cc - 2, 0))])
                t3 = r.emit("tensor", "transpose",
                            dict(out=PTb, in_=vh[:, cs], identity=consts[:, 0:128]))
                vS1 = r.emit("vector", "tensor_tensor",
                             dict(out=sb_, in0=SC_PS, in1=mh[:, :], op=ALU.mult),
                             waits=[("sp", t1)])
                vS1_of[cc] = vS1
                vq = r.emit("vector", "tensor_tensor",
                            dict(out=qc, in0=qh[:, cs], in1=rh[:, :], op=ALU.mult),
                            waits=[("sv", vS1), ("sp", t5_of.get(cc - 2, 0))])
                vk = r.emit("vector", "tensor_scalar",
                            dict(out=kt, in0=PTa, scalar1=c_(CO_KD + h),
                                 scalar2=None, op0=ALU.mult),
                            waits=[("sp", t3), ("sv", vq)])
                vk_of[cc] = vk
                vv = r.emit("vector", "tensor_copy", dict(out=vt, in_=PTb),
                            waits=[("sv", vk)])
                t4 = r.emit("tensor", "matmul",
                            dict(out=O_PS, lhsT=sb_, rhs=vt,
                                 start=True, stop=False, skip_group_check=True),
                            waits=[("sv", vv), ("sa", say_of.get(cc - 2, 0))])
                t4_of[cc] = t4
                t5 = r.emit("tensor", "matmul",
                            dict(out=O_PS, lhsT=qc, rhs=sbf16[:, :],
                                 start=False, stop=True, skip_group_check=True),
                            waits=[("sa", sa0_of[cc - 1])])
                t5_of[cc] = t5
                say = r.emit("scalar", "copy",
                             dict(out=A2[:, cc * D + h * 128: cc * D + (h + 1) * 128],
                                  in_=O_PS), waits=[("sp", t5)])
                say_of[cc] = say
                t6 = r.emit("tensor", "matmul",
                            dict(out=DS_PS, lhsT=kt, rhs=vt,
                                 start=True, stop=True, skip_group_check=True),
                            waits=[("sv", vS2_of.get(cc - 2, 0))])
                t6_of[cc] = t6
                vS2 = r.emit("vector", "scalar_tensor_tensor",
                             dict(out=Sst[:, h * 128:(h + 1) * 128],
                                  in0=Sst[:, h * 128:(h + 1) * 128],
                                  scalar=float(gammas[h] ** C), in1=DS_PS,
                                  op0=ALU.mult, op1=ALU.add),
                             waits=[("sp", t6)])
                vS2_of[cc] = vS2
                sa0 = r.emit("scalar", "copy",
                             dict(out=sbf16[:, :], in_=Sst[:, h * 128:(h + 1) * 128]),
                             waits=[("sv", vS2), ("sp", t5)])
                sa0_of[cc] = sa0
            sp_mix[h] = r.val("sp")
        barrier()

        for t in range(NT):
            src = A2[:, t * D:(t + 1) * D]
            vS = ln_stats(src, D)
            v1 = r.emit("vector", "tensor_scalar",
                        dict(out=ta[:, :], in0=src, scalar1=c_(CO_MEAN), scalar2=c_(CO_RSTD),
                             op0=ALU.subtract, op1=ALU.mult), waits=[("sv", vS)])
            v1 = r.emit("vector", "tensor_tensor",
                        dict(out=ta[:, :], in0=ta[:, :], in1=wsb[:, 4096:4096 + D], op=ALU.mult),
                        waits=[("sv", v1)])
            v1 = r.emit("vector", "tensor_tensor",
                        dict(out=src, in0=ta[:, :], in1=wsb[:, 4096 + D:4096 + 2 * D], op=ALU.add),
                        waits=[("sv", v1)])
            r.emit("sync", "dma_start",
                   dict(out=yln_s[t * 128:(t + 1) * 128, :], in_=src),
                   waits=[("sv", v1)])
        barrier()

        for j in range(FC):
            wgd = load_wcol(wsb[:, (j % 2) * D:(j % 2) * D + D], w_attn["wg"], j,
                            waits=[("sp", r.val("sp"))])
            blks = [(i, min(512, T - i)) for i in range(0, T, 512)]
            for bi, (b0, bw) in enumerate(blks):
                for k in range(FC):
                    r.emit("tensor", "matmul",
                           dict(out=PA[:, b0:b0 + bw],
                                lhsT=wsb[:, (j % 2) * D + k * 128:(j % 2) * D + (k + 1) * 128],
                                rhs=A1[:, k * T + b0: k * T + b0 + bw],
                                start=(k == 0), stop=(k == FC - 1)),
                           waits=([("dg", wgd), ("sa", r.val("sa")), ("sv", r.val("sv"))]
                                  if (bi == 0 and k == 0) else []),
                           inc=(("sp", 1) if (k == FC - 1 and bi == len(blks) - 1) else False))
            r.emit("scalar", "activation",
                   dict(out=A2[:, j * T:(j + 1) * T], in_=PA[:, 0:T], func=AF.Silu),
                   waits=[("sp", r.val("sp"))])
        barrier()
        for t in range(NT):
            r.emit("sync", "dma_start",
                   dict(out=A1[:, t * D:(t + 1) * D], in_=yln_s[t * 128:(t + 1) * 128, :]),
                   waits=([("dq", r.val("dq"))] if t == 0 else []))
        barrier()
        for j in range(FC):
            for g0 in range(0, NT, 8):
                gn = min(8, NT - g0)
                last = None
                for tt_ in range(gn):
                    t = g0 + tt_
                    last = r.emit("tensor", "transpose",
                                  dict(out=PT[:, tt_ * 128:(tt_ + 1) * 128],
                                       in_=A1[:, t * D + j * 128: t * D + (j + 1) * 128],
                                       identity=consts[:, 0:128]),
                                  waits=([("sv", r.val("sv"))] if tt_ == 0 else []),
                                  inc=(("sp", 1) if tt_ == gn - 1 else False))
                r.emit("vector", "tensor_tensor",
                       dict(out=A2[:, j * T + g0 * 128: j * T + (g0 + gn) * 128],
                            in0=PT[:, 0:gn * 128],
                            in1=A2[:, j * T + g0 * 128: j * T + (g0 + gn) * 128],
                            op=ALU.mult),
                       waits=[("sp", last)])
        barrier()

        for j in range(FC):
            wod = load_wcol(wsb[:, (j % 2) * D:(j % 2) * D + D], w_attn["wo"], j,
                            waits=[("sp", r.val("sp"))])
            blks = [(i, min(512, T - i)) for i in range(0, T, 512)]
            for bi, (b0, bw) in enumerate(blks):
                for k in range(FC):
                    r.emit("tensor", "matmul",
                           dict(out=PA[:, b0:b0 + bw],
                                lhsT=wsb[:, (j % 2) * D + k * 128:(j % 2) * D + (k + 1) * 128],
                                rhs=A2[:, k * T + b0: k * T + b0 + bw],
                                start=(k == 0), stop=(k == FC - 1)),
                           waits=([("dg", wod), ("sa", r.val("sa")), ("sv", r.val("sv"))]
                                  if (bi == 0 and k == 0) else []),
                           inc=(("sp", 1) if (k == FC - 1 and bi == len(blks) - 1) else False))
            r.emit("scalar", "copy",
                   dict(out=A1[:, j * T:(j + 1) * T], in_=PA[:, 0:T]),
                   waits=[("sp", r.val("sp"))])
        barrier()

        for t in range(NT):
            for g0 in range(0, FC, 8):
                gn = min(8, FC - g0)
                last = None
                for jj in range(gn):
                    j = g0 + jj
                    last = r.emit("tensor", "transpose",
                                  dict(out=PT[:, jj * 128:(jj + 1) * 128],
                                       in_=A1[:, j * T + t * 128: j * T + (t + 1) * 128],
                                       identity=consts[:, 0:128]),
                                  waits=([("sv", r.val("sv")), ("sa", r.val("sa"))]
                                         if jj == 0 else []),
                                  inc=(("sp", 1) if jj == gn - 1 else False))
                r.emit("scalar", "copy",
                       dict(out=ta[:, g0 * 128:(g0 + gn) * 128], in_=PT[:, 0:gn * 128]),
                       waits=[("sp", last)])
            r.emit("vector", waits=[("sa", r.val("sa"))])
            vS = ln_stats(ta[:, :], D, scratch=xt[:, :])
            v1 = r.emit("vector", "tensor_scalar",
                        dict(out=ta[:, :], in0=ta[:, :], scalar1=c_(CO_MEAN),
                             scalar2=c_(CO_RSTD), op0=ALU.subtract, op1=ALU.mult),
                        waits=[("sv", vS)])
            d1 = r.emit("sync", "dma_start",
                        dict(out=xt[:, :], in_=x_own[t * 128:(t + 1) * 128, :]),
                        waits=[("sv", r.val("sv")), ("dq", r.val("dq")),
                               ("dst", r.val("dst"))])
            v2 = r.emit("vector", "tensor_tensor",
                        dict(out=xt[:, :], in0=ta[:, :], in1=xt[:, :], op=ALU.add),
                        waits=[("sv", v1), ("dq", d1)])
            r.emit("sync", "dma_start",
                   dict(out=x2_s[t * 128:(t + 1) * 128, :], in_=xt[:, :]),
                   waits=[("sv", v2)])
            vS = ln_stats(xt[:, :], D)
            v3 = r.emit("vector", "tensor_scalar",
                        dict(out=ta[:, :], in0=xt[:, :], scalar1=c_(CO_MEAN),
                             scalar2=c_(CO_RSTD), op0=ALU.subtract, op1=ALU.mult),
                        waits=[("sv", vS)])
            v4 = r.emit("vector", "tensor_reduce",
                        dict(out=c_(CO_TMP), in_=ta[:, :], axis=AX.X, op=ALU.max,
                             apply_absolute_value=True), waits=[("sv", v3)])
            v4 = r.emit("vector", "tensor_scalar",
                        dict(out=c_(CO_OS + t), in0=c_(CO_TMP), scalar1=float(QEPS),
                             scalar2=float(1.0 / 128.0), op0=ALU.max, op1=ALU.mult),
                        waits=[("sv", v4)])
            v4 = r.emit("vector", "reciprocal", dict(out=c_(CO_SC + t), in_=c_(CO_OS + t)),
                        waits=[("sv", v4)])
            v5 = r.emit("vector", "tensor_scalar",
                        dict(out=ta[:, :], in0=ta[:, :], scalar1=c_(CO_SC + t), scalar2=c_(CO_A),
                             op0=ALU.mult, op1=ALU.add), waits=[("sv", v4)])
            v5 = r.emit("vector", "tensor_scalar",
                        dict(out=ta[:, :], in0=ta[:, :], scalar1=c_(CO_A), scalar2=127.0,
                             op0=ALU.subtract, op1=ALU.min), waits=[("sv", v5)])
            v5 = r.emit("vector", "tensor_scalar",
                        dict(out=wsb[:, 0:D], in0=ta[:, :], scalar1=-128.0, scalar2=None,
                             op0=ALU.max), waits=[("sv", v5)])
            transpose_tile(
                lambda k, _t=t: A2[:, k * T + _t * 128: k * T + (_t + 1) * 128],
                wsb[:, 0:D], [("sv", v5)])
        barrier()

        wm_cons = {}

        def wmean_block(src, acc_idx, i, width=None):
            wd = D if width is None else width
            par = i % 2
            qs = "dq" if par == 0 else "dg"
            eng = "sync" if par == 0 else "gpsimd"
            stage = xt[:, 0:wd] if par == 0 else fbuf[:, 0:wd]
            d_ = r.emit(eng, "dma_start",
                        dict(out=stage.rearrange("p (k c) -> p k c", c=128), in_=src),
                        waits=([wm_cons[i - 2]] if (i - 2) in wm_cons else
                               [("sv", r.val("sv")), ("sa", r.val("sa"))]),
                        inc=(qs, 16))
            if par == 0:
                r.emit("vector", waits=[(qs, d_)])
                c2 = r.emit("vector", "tensor_reduce",
                            dict(out=c_(CO_ACC + acc_idx), in_=stage, axis=AX.X,
                                 op=ALU.add, apply_absolute_value=True))
                wm_cons[i] = ("sv", c2)
            else:
                r.emit("scalar", waits=[(qs, d_)])
                c2 = r.emit("scalar", "activation",
                            dict(out=ta[:, 0:wd], in_=stage, func=AF.Abs,
                                 accum_out=c_(CO_ACC + acc_idx)))
                wm_cons[i] = ("sa", c2)

        for i in range(IC):
            wmean_block(wgate_bc[i].rearrange("(k p) c -> p k c", p=128), i, i)
        for i in range(IC):
            wmean_block(wup_bc[i].rearrange("(k p) c -> p k c", p=128), IC + i, IC + i)
        for i in range(FC * 4):
            col, piece = i // 4, i % 4
            wmean_block(wdown_bc[col][piece * (I // 4):(piece + 1) * (I // 4)]
                        .rearrange("(k p) c -> p k c", p=128), 2 * IC + i, 2 * IC + i,
                        width=I // 4)
        barrier()
        nacc_g, nacc_u, nacc_d = IC, IC, FC * 4
        v6 = r.emit("vector", "tensor_reduce",
                    dict(out=c_(CO_TMP), in_=cols[:, CO_ACC:CO_ACC + nacc_g], axis=AX.X,
                         op=ALU.add))
        v6 = r.emit("vector", "tensor_reduce",
                    dict(out=cols[:, CO_TMP + 1:CO_TMP + 2],
                         in_=cols[:, CO_ACC + nacc_g:CO_ACC + nacc_g + nacc_u], axis=AX.X,
                         op=ALU.add), waits=[("sv", v6)])
        v6 = r.emit("vector", "tensor_reduce",
                    dict(out=cols[:, CO_TMP + 2:CO_TMP + 3],
                         in_=cols[:, CO_ACC + nacc_g + nacc_u:
                                  CO_ACC + nacc_g + nacc_u + nacc_d], axis=AX.X,
                         op=ALU.add), waits=[("sv", v6)])
        d7 = r.emit("sync", "dma_start",
                    dict(out=bounce[0:1, 0:384], in_=cols[:, CO_TMP:CO_TMP + 3]),
                    waits=[("sv", v6)])
        d8 = r.emit("sync", "dma_start",
                    dict(out=rowbuf[0:1, 0:384], in_=bounce[0:1, 0:384]),
                    waits=[("dq", d7)])
        r.emit("vector", waits=[("dq", d8)])
        vm = r.val("sv")
        for wi in range(3):
            vm = r.emit("vector", "tensor_reduce",
                        dict(out=rowbuf[0:1, 384 + wi:385 + wi],
                             in_=rowbuf[0:1, 0:384].rearrange("o (p c) -> o c p", c=3)[:, wi, :],
                             axis=AX.X, op=ALU.add),
                        waits=[("sv", vm)])
        numel = float(I * D)
        for wi in range(3):
            vm = r.emit("vector", "tensor_scalar",
                        dict(out=rowbuf[0:1, 388 + wi:389 + wi],
                             in0=rowbuf[0:1, 384 + wi:385 + wi],
                             scalar1=float(1.0 / numel), scalar2=float(QEPS),
                             op0=ALU.mult, op1=ALU.max), waits=[("sv", vm)])
        vm = r.emit("vector", "reciprocal",
                    dict(out=rowbuf[0:1, 392:395], in_=rowbuf[0:1, 388:391]),
                    waits=[("sv", vm)])
        t9 = r.emit("tensor", "matmul",
                    dict(out=PS[:, 0:3], lhsT=onesr[:, :], rhs=rowbuf[0:1, 388:391],
                         start=True, stop=True),
                    waits=[("sv", vm), ("sa", r.val("sa"))])
        vm2 = r.emit("vector", "tensor_copy", dict(out=cols[:, CO_MW:CO_MW + 3], in_=PS[:, 0:3]),
                     waits=[("sp", t9)])
        vh1 = r.emit("vector", "tensor_copy", dict(out=rowsb[0:1, 0:3], in_=rowbuf[0:1, 392:395]),
                     waits=[("sv", vm2)])
        vh2 = r.emit("vector", "tensor_tensor",
                     dict(out=rowbuf[0:1, 396:399], in0=rowbuf[0:1, 392:395],
                          in1=rowsb[0:1, 0:3], op=ALU.subtract), waits=[("sv", vh1)])
        vh3 = r.emit("vector", "tensor_copy", dict(out=rowsb[0:1, 4:7], in_=rowbuf[0:1, 396:399]),
                     waits=[("sv", vh2)])
        vh4 = r.emit("vector", "tensor_tensor",
                     dict(out=rowbuf[0:1, 396:399], in0=rowbuf[0:1, 396:399],
                          in1=rowsb[0:1, 4:7], op=ALU.subtract), waits=[("sv", vh3)])
        vh5 = r.emit("vector", "tensor_copy", dict(out=rowsb[0:1, 8:11], in_=rowbuf[0:1, 396:399]),
                     waits=[("sv", vh4)])
        tw1 = r.emit("tensor", "matmul",
                     dict(out=PS[:, 8:11], lhsT=onesb[:, :], rhs=rowsb[0:1, 0:3],
                          start=True, stop=False, skip_group_check=True),
                     waits=[("sv", vh5), ("sa", r.val("sa"))], inc=False)
        tw1 = r.emit("tensor", "matmul",
                     dict(out=PS[:, 8:11], lhsT=onesb[:, :], rhs=rowsb[0:1, 4:7],
                          start=False, stop=False, skip_group_check=True), inc=False)
        tw1 = r.emit("tensor", "matmul",
                     dict(out=PS[:, 8:11], lhsT=onesb[:, :], rhs=rowsb[0:1, 8:11],
                          start=False, stop=True, skip_group_check=True))
        r.emit("vector", "tensor_copy", dict(out=cols[:, CO_WS:CO_WS + 3], in_=PS[:, 8:11]),
               waits=[("sp", tw1)])
        barrier()

        tn_cons = {}
        tn_store = {}

        def tern_block(src, dst, ws_idx, i, width=None):
            wd = D if width is None else width
            par = i % 2
            qs = "dq" if par == 0 else "dg"
            ss = "dg2" if par == 0 else "dq3"
            stage = xt[:, 0:wd] if par == 0 else fbuf[:, 0:wd]
            oslot = wsb[:, 0:wd] if par == 0 else wsb[:, 2048:2048 + wd]
            d_ = r.emit("sync" if par == 0 else "gpsimd", "dma_start",
                        dict(out=stage.rearrange("p (k c) -> p k c", c=128), in_=src),
                        waits=([tn_cons[i - 2]] if (i - 2) in tn_cons else
                               [("sv", r.val("sv")), ("sa", r.val("sa"))]),
                        inc=(qs, 16))
            v_ = r.emit("vector", "tensor_scalar",
                        dict(out=ta[:, 0:wd], in0=stage, scalar1=c_(CO_WS + ws_idx),
                             scalar2=c_(CO_A), op0=ALU.mult, op1=ALU.add),
                        waits=[(qs, d_), ("sv", r.val("sv"))])
            tn_cons[i] = ("sv", v_)
            v_ = r.emit("vector", "tensor_scalar",
                        dict(out=ta[:, 0:wd], in0=ta[:, 0:wd], scalar1=c_(CO_A), scalar2=1.0,
                             op0=ALU.subtract, op1=ALU.min), waits=[("sv", v_)])
            v_ = r.emit("vector", "tensor_scalar",
                        dict(out=oslot, in0=ta[:, 0:wd], scalar1=-1.0, scalar2=None,
                             op0=ALU.max),
                        waits=[("sv", v_), (ss, tn_store.get(i - 2, 0))])
            tn_store[i] = r.emit("gpsimd" if par == 0 else "sync", "dma_start",
                                 dict(out=dst, in_=oslot.rearrange("p (k c) -> p k c", c=128)),
                                 waits=[("sv", v_)], inc=(ss, 16))

        ti = 0
        for i in range(IC):
            tern_block(wgate_bc[i].rearrange("(k p) c -> p k c", p=128),
                       terng_s[i].rearrange("(k p) c -> p k c", p=128), 0, ti)
            ti += 1
        for i in range(IC):
            tern_block(wup_bc[i].rearrange("(k p) c -> p k c", p=128),
                       ternu_s[i].rearrange("(k p) c -> p k c", p=128), 1, ti)
            ti += 1
        for i in range(FC * 4):
            col, piece = i // 4, i % 4
            tern_block(wdown_bc[col][piece * (I // 4):(piece + 1) * (I // 4)]
                       .rearrange("(k p) c -> p k c", p=128),
                       ternd_s[col][piece * (I // 4):(piece + 1) * (I // 4)]
                       .rearrange("(k p) c -> p k c", p=128), 2, ti, width=I // 4)
            ti += 1
        barrier()

        SBG = fbuf[:, 0:G]
        SBU = fbuf[:, G:2 * G]
        SBQ = fbuf[:, 2 * G:3 * G]
        SBD = fbuf[:, 3 * G:4 * G]
        for s in range(NSUP):
            t0 = s * GT
            for tt_ in range(GT):
                sa1 = row_from_col(c_(CO_OS + t0 + tt_), 0)
                vb = bcast_row(0, SBG[:, tt_ * 128:(tt_ + 1) * 128])
                vb = r.emit("vector", "tensor_scalar",
                            dict(out=SBU[:, tt_ * 128:(tt_ + 1) * 128],
                                 in0=SBG[:, tt_ * 128:(tt_ + 1) * 128],
                                 scalar1=c_(CO_MW + 1), scalar2=None, op0=ALU.mult),
                            waits=[("sv", vb)])
                vb = r.emit("vector", "tensor_scalar",
                            dict(out=SBG[:, tt_ * 128:(tt_ + 1) * 128],
                                 in0=SBG[:, tt_ * 128:(tt_ + 1) * 128],
                                 scalar1=c_(CO_MW + 0), scalar2=None, op0=ALU.mult),
                            waits=[("sv", vb)])
            barrier()
            sp_mms = {}
            t8_of = {}
            g8_of = {}

            def gu_tail(jo):
                par = jo % 2
                PSG = PB[:, 0:G] if par == 0 else PA[:, 0:G]
                PSU = PB[:, 512:512 + G] if par == 0 else PA[:, 512:512 + G]
                PSH = PS[:, 0:G] if par == 0 else PA[:, 1024:1024 + G]
                v7 = r.emit("vector", "tensor_tensor",
                            dict(out=ta[:, 0:G], in0=PSG, in1=SBG, op=ALU.mult),
                            waits=[("sp", sp_mms[jo]), ("sv", g8_of.get(jo - 1, 0))])
                a7 = r.emit("scalar", "activation",
                            dict(out=ta[:, 0:G], in_=ta[:, 0:G], func=AF.Silu),
                            waits=[("sv", v7)])
                v8 = r.emit("vector", "tensor_tensor",
                            dict(out=ta[:, G:2 * G], in0=PSU, in1=SBU, op=ALU.mult),
                            waits=[("sv", v7)])
                g8 = r.emit("vector", "tensor_tensor",
                            dict(out=sb512[:, 0:G], in0=ta[:, 0:G], in1=ta[:, G:2 * G],
                                 op=ALU.mult),
                            waits=[("sv", v8), ("sa", a7),
                                   ("sp", t8_of.get(jo - 1, 0))])
                g8_of[jo] = g8
                t8 = r.emit("tensor", "matmul",
                            dict(out=PSH, lhsT=consts[:, 128:256], rhs=sb512[:, 0:G],
                                 start=True, stop=True, skip_group_check=True),
                            waits=[("sv", g8), ("sa", r.val("sa"))])
                t8_of[jo] = t8
                r.emit("scalar", "copy",
                       dict(out=A1[:, jo * G:(jo + 1) * G], in_=PSH),
                       waits=[("sp", t8)])

            for jo in range(IC):
                par = jo % 2
                qsem = "dq" if par == 0 else "dq2"
                wgd = r.emit("sync", "dma_start",
                             dict(out=wsb[:, par * 2 * D: par * 2 * D + D]
                                  .rearrange("p (k c) -> p k c", c=128),
                                  in_=terng_s[jo].rearrange("(k p) c -> p k c", p=128)),
                             waits=([("sp", sp_mms.get(jo - 2, 0))] if jo >= 2 else []),
                             inc=(qsem, 16))
                wud = r.emit("sync", "dma_start",
                             dict(out=wsb[:, par * 2 * D + D: par * 2 * D + 2 * D]
                                  .rearrange("p (k c) -> p k c", c=128),
                                  in_=ternu_s[jo].rearrange("(k p) c -> p k c", p=128)),
                             inc=(qsem, 16))
                qtot = r.val(qsem)
                PSG = PB[:, 0:G] if par == 0 else PA[:, 0:G]
                PSU = PB[:, 512:512 + G] if par == 0 else PA[:, 512:512 + G]
                for k in range(FC):
                    r.emit("tensor", "matmul",
                           dict(out=PSG,
                                lhsT=wsb[:, par * 2 * D + k * 128: par * 2 * D + (k + 1) * 128],
                                rhs=A2[:, k * T + t0 * 128: k * T + t0 * 128 + G],
                                start=(k == 0), stop=(k == FC - 1), skip_group_check=True),
                           waits=([(qsem, qtot), ("sv", r.val("sv")), ("sa", r.val("sa"))]
                                  if k == 0 else []),
                           inc=False)
                for k in range(FC):
                    r.emit("tensor", "matmul",
                           dict(out=PSU,
                                lhsT=wsb[:, par * 2 * D + D + k * 128:
                                         par * 2 * D + D + (k + 1) * 128],
                                rhs=A2[:, k * T + t0 * 128: k * T + t0 * 128 + G],
                                start=(k == 0), stop=(k == FC - 1), skip_group_check=True),
                           inc=(("sp", 1) if k == FC - 1 else False))
                sp_mms[jo] = r.val("sp")
                if jo >= 1:
                    gu_tail(jo - 1)
            gu_tail(IC - 1)
            barrier()
            for st in range(int(math.log2(IC))):
                step = 1 << st
                svS, sgS = r.val("sv"), r.val("sg")
                pi = 0
                for base in range(0, IC, 2 * step):
                    for off in range(step):
                        ia, ib = base + off, base + off + step
                        aslc = A1[:, ia * G:(ia + 1) * G]
                        bslc = A1[:, ib * G:(ib + 1) * G]
                        pi += 1
                        v9 = r.emit("vector", "tensor_tensor",
                                    dict(out=bslc, in0=aslc, in1=bslc, op=ALU.subtract),
                                    waits=[("sv", svS), ("sa", r.val("sa"))])
                        r.emit("vector", "scalar_tensor_tensor",
                               dict(out=aslc, in0=aslc, scalar=2.0, in1=bslc,
                                    op0=ALU.mult, op1=ALU.subtract),
                               waits=[("sv", v9)])
            barrier()
            svS, sgS = r.val("sv"), r.val("sg")
            vq = r.emit("vector", "tensor_tensor",
                        dict(out=ta[:, 0:G], in0=A1[:, 0:G], in1=A1[:, G:2 * G],
                             op=ALU.max), waits=[("sv", svS), ("sg", sgS)])
            gq = r.emit("vector", "tensor_tensor",
                        dict(out=xt[:, 0:G], in0=A1[:, 0:G], in1=A1[:, G:2 * G],
                             op=ALU.min),
                        waits=[("sv", svS), ("dst", r.val("dst"))])
            for jo in range(2, IC):
                vq = r.emit("vector", "tensor_tensor",
                            dict(out=ta[:, 0:G], in0=ta[:, 0:G],
                                 in1=A1[:, jo * G:(jo + 1) * G], op=ALU.max),
                            waits=[("sv", vq)])
                gq = r.emit("vector", "tensor_tensor",
                            dict(out=xt[:, 0:G], in0=xt[:, 0:G],
                                 in1=A1[:, jo * G:(jo + 1) * G], op=ALU.min),
                            waits=[("sv", gq)])
            vq = r.emit("vector", "tensor_scalar",
                        dict(out=xt[:, G:2 * G] if 2 * G <= D else xt[:, 0:G], in0=xt[:, 0:G],
                             scalar1=-1.0, scalar2=None, op0=ALU.mult),
                        waits=[("sv", gq), ("sv", vq)])
            vq = r.emit("vector", "tensor_tensor",
                        dict(out=ta[:, 0:G], in0=ta[:, 0:G],
                             in1=xt[:, G:2 * G] if 2 * G <= D else xt[:, 0:G],
                             op=ALU.max), waits=[("sv", vq)])
            vqb = r.emit("vector", "tensor_copy", dict(out=sb512[:, 0:G], in_=ta[:, 0:G]),
                         waits=[("sv", vq)])
            for tt_ in range(GT):
                tq1 = r.emit("tensor", "transpose",
                             dict(out=PT[:, 0:128], in_=sb512[:, tt_ * 128:(tt_ + 1) * 128],
                                  identity=consts[:, 0:128]),
                             waits=[("sv", vqb), ("sa", r.val("sa")), ("sv", r.val("sv"))])
                vq2 = r.emit("vector", "tensor_reduce",
                             dict(out=c_(CO_TMP), in_=PT[:, 0:128], axis=AX.X, op=ALU.max),
                             waits=[("sp", tq1)])
                vq2 = r.emit("vector", "tensor_scalar",
                             dict(out=c_(CO_TMP), in0=c_(CO_TMP), scalar1=float(FWC),
                                  scalar2=float(QEPS), op0=ALU.mult, op1=ALU.max),
                             waits=[("sv", vq2)])
                vq3 = r.emit("vector", "tensor_scalar",
                             dict(out=c_(CO_OSD + tt_), in0=c_(CO_TMP),
                                  scalar1=float(1.0 / 128.0), scalar2=c_(CO_MW + 2),
                                  op0=ALU.mult, op1=ALU.mult), waits=[("sv", vq2)])
                vq4 = r.emit("vector", "tensor_scalar",
                             dict(out=c_(CO_SCD + tt_), in0=c_(CO_TMP),
                                  scalar1=float(1.0 / (FWC * 128.0)), scalar2=None,
                                  op0=ALU.mult), waits=[("sv", vq3)])
                vq4 = r.emit("vector", "reciprocal",
                             dict(out=c_(CO_SCD + tt_), in_=c_(CO_SCD + tt_)),
                             waits=[("sv", vq4)])
                sa1 = row_from_col(c_(CO_SCD + tt_), 0)
                bcast_row(0, SBQ[:, tt_ * 128:(tt_ + 1) * 128])
                sa1 = row_from_col(c_(CO_OSD + tt_), 128)
                bcast_row(128, SBD[:, tt_ * 128:(tt_ + 1) * 128])
            svS, sgS = r.val("sv"), r.val("sg")
            for jo in range(IC):
                aslc = A1[:, jo * G:(jo + 1) * G]
                m5 = r.emit("vector", "tensor_tensor",
                            dict(out=ta[:, 0:G], in0=aslc, in1=SBQ, op=ALU.mult),
                            waits=[("sv", r.val("sv"))])
                wv_ = ("sv", m5)
                scr = ta[:, 0:G]
                vq5 = r.emit("vector", "tensor_scalar",
                             dict(out=scr, in0=scr, scalar1=c_(CO_A),
                                  scalar2=None, op0=ALU.add), waits=[wv_])
                vq5 = r.emit("vector", "tensor_scalar",
                             dict(out=scr, in0=scr, scalar1=c_(CO_A),
                                  scalar2=127.0, op0=ALU.subtract, op1=ALU.min),
                             waits=[("sv", vq5)])
                r.emit("vector", "tensor_scalar",
                       dict(out=aslc, in0=scr, scalar1=-128.0, scalar2=None,
                            op0=ALU.max), waits=[("sv", vq5)])
            barrier()
            ICH = IC // 2
            dn_mid = {}
            dn_end = {}
            st3 = {}
            sad_of = {}

            def dn_load(fo, half, waits):
                return r.emit("sync", "dma_start",
                              dict(out=wsb[:, half * 4096: half * 4096 + ICH * 128]
                                   .rearrange("p (k c) -> p k c", c=128),
                                   in_=ternd_s[fo][half * (I // 2):(half + 1) * (I // 2)]
                                   .rearrange("(k p) c -> p k c", p=128)),
                              waits=waits, inc=("dqa" if half == 0 else "dqb", 16))

            tp_of = {}

            def dn_tail(fo):
                par = fo % 2
                PSD = PA[:, 0:G] if par == 0 else PA[:, 512:512 + G]
                vd = r.emit("vector", "tensor_tensor",
                            dict(out=sb512[:, G:2 * G], in0=PSD, in1=SBD, op=ALU.mult),
                            waits=[("sp", max(dn_end[fo], tp_of.get(fo - 1, 0))),
                                   ("dq3", st3.get(fo - 1, 0))])
                last = None
                for tt_ in range(GT):
                    last = r.emit("tensor", "transpose",
                                  dict(out=PT[:, tt_ * 128:(tt_ + 1) * 128],
                                       in_=sb512[:, G + tt_ * 128: G + (tt_ + 1) * 128],
                                       identity=consts[:, 0:128]),
                                  waits=([("sv", vd), ("sa", r.val("sa"))] if tt_ == 0 else []),
                                  inc=(("sp", 1) if tt_ == GT - 1 else False))
                tp_of[fo] = r.val("sp")
                sad = r.emit("scalar", "copy",
                             dict(out=sb512[:, 0:G], in_=PT[:, 0:G]),
                             waits=[("sp", last), ("dq3", st3.get(fo - 1, 0))])
                sad_of[fo] = sad
                st3[fo] = r.emit("sync", "dma_start",
                                 dict(out=dfm_s[fo].rearrange("t p c -> p t c"),
                                      in_=sb512[:, 0:G].rearrange("p (t c) -> p t c", c=128)),
                                 waits=[("sa", sad)], inc=("dq3", 16))

            dn_load(0, 0, [("sv", r.val("sv")), ("sg", r.val("sg")), ("sa", r.val("sa"))])
            dn_load(0, 1, [])
            for fo in range(FC):
                par = fo % 2
                PSD = PA[:, 0:G] if par == 0 else PA[:, 512:512 + G]
                ta_tot, tb_tot = r.val("dqa"), r.val("dqb")
                for k in range(ICH):
                    r.emit("tensor", "matmul",
                           dict(out=PSD,
                                lhsT=wsb[:, (k % ICH) * 128:(k % ICH) * 128 + 128],
                                rhs=A1[:, k * G:(k + 1) * G],
                                start=(k == 0), stop=False, skip_group_check=True),
                           waits=([("dqa", ta_tot), ("sv", r.val("sv")),
                                   ("sg", r.val("sg")), ("sa", r.val("sa"))]
                                  if k == 0 else []),
                           inc=(("sp", 1) if k == ICH - 1 else False))
                dn_mid[fo] = r.val("sp")
                if fo + 1 < FC:
                    dn_load(fo + 1, 0, [("sp", dn_mid[fo])])
                for k in range(ICH, IC):
                    r.emit("tensor", "matmul",
                           dict(out=PSD,
                                lhsT=wsb[:, 4096 + (k % ICH) * 128: 4096 + (k % ICH) * 128 + 128],
                                rhs=A1[:, k * G:(k + 1) * G],
                                start=False, stop=(k == IC - 1), skip_group_check=True),
                           waits=([("dqb", tb_tot)] if k == ICH else []),
                           inc=(("sp", 1) if k == IC - 1 else False))
                dn_end[fo] = r.val("sp")
                if fo + 1 < FC:
                    dn_load(fo + 1, 1, [("sp", dn_end[fo])])
                if fo >= 1:
                    dn_tail(fo - 1)
            dn_tail(FC - 1)
            barrier()
            for tt_ in range(GT):
                t = t0 + tt_
                dg = r.emit("sync", "dma_start",
                            dict(out=wsb[:, 0:D].rearrange("p (k c) -> p k c", c=128),
                                 in_=dfm_s[:, tt_].rearrange("k p c -> p k c")),
                            waits=[("sv", r.val("sv")), ("sa", r.val("sa")),
                                   ("dq", r.val("dq")), ("dst", r.val("dst"))])
                sc = r.emit("scalar", "copy", dict(out=ta[:, :], in_=wsb[:, 0:D]),
                            waits=[("dq", dg)])
                r.emit("vector", waits=[("sa", sc)])
                vS = ln_stats(ta[:, :], D, scratch=xt[:, :])
                vf = r.emit("vector", "tensor_scalar",
                            dict(out=ta[:, :], in0=ta[:, :], scalar1=c_(CO_MEAN),
                                 scalar2=c_(CO_RSTD), op0=ALU.subtract, op1=ALU.mult),
                            waits=[("sv", vS)])
                df = r.emit("sync", "dma_start",
                            dict(out=xt[:, :], in_=x2_s[t * 128:(t + 1) * 128, :]),
                            waits=[("sv", r.val("sv")), ("dst", r.val("dst"))])
                vf = r.emit("vector", "tensor_tensor",
                            dict(out=xt[:, :], in0=ta[:, :], in1=xt[:, :], op=ALU.add),
                            waits=[("sv", vf), ("dq", df)])
                r.emit("sync", "dma_start",
                       dict(out=out[t * 128:(t + 1) * 128, :], in_=xt[:, :]),
                       waits=[("sv", vf)], inc=("dst", 16))
        r.emit("sync", waits=[("dst", r.val("dst"))])

        for ename in ENGINES:
            def mk(en):
                def f(eng):
                    r.replay(en, eng, sems)
                return f
            getattr(block, ename)(mk(ename))

    return nc


def _prep_inputs(L0_attn_x, L0_attn_wq, L0_attn_wk, L0_attn_wv, L0_attn_wg, L0_attn_wo,
                 L0_attn_ln_w, L0_attn_ln_b, L0_ffn_w_gate, L0_ffn_w_up, L0_ffn_w_down):
    D, I = 2048, 8192
    FC, IC = D // 128, I // 128

    def bc(wT, nblk):
        fi, fo = wT.shape
        return np.ascontiguousarray(wT.reshape(fi, nblk, 128).transpose(1, 0, 2))

    w = {}
    for nm, mat in (("wq", L0_attn_wq), ("wk", L0_attn_wk), ("wv", L0_attn_wv),
                    ("wg", L0_attn_wg), ("wo", L0_attn_wo)):
        w[nm + "_bc"] = bc(np.ascontiguousarray(np.asarray(mat, np.float32).T), FC)
    w["wgate_bc"] = bc(np.ascontiguousarray(np.asarray(L0_ffn_w_gate, np.float32).T), IC)
    w["wup_bc"] = bc(np.ascontiguousarray(np.asarray(L0_ffn_w_up, np.float32).T), IC)
    w["wdown_bc"] = bc(np.ascontiguousarray(np.asarray(L0_ffn_w_down, np.float32).T), FC)
    w["ln_w"] = np.asarray(L0_attn_ln_w, np.float32).reshape(1, D)
    w["ln_b"] = np.asarray(L0_attn_ln_b, np.float32).reshape(1, D)

    x = np.asarray(L0_attn_x, np.float32)
    in_maps = []
    for core in range(8):
        b, hh = core // 2, core % 2
        m = dict(w)
        m["x_own"] = np.ascontiguousarray(x[b, hh * 2048:(hh + 1) * 2048, :])
        m["x_pref"] = np.ascontiguousarray(x[b, 0:2048, :])
        m["maskin"] = np.array([[1.0 if hh == 1 else 0.0]], np.float32)
        in_maps.append(m)
    return in_maps


def kernel(**inputs):
    in_maps = _prep_inputs(**inputs)
    nc = build_program()
    res = run_bass_kernel_spmd(nc, in_maps, core_ids=list(range(8)))
    full = np.empty((4, 4096, 2048), np.float32)
    for core in range(8):
        b, hh = core // 2, core % 2
        full[b, hh * 2048:(hh + 1) * 2048, :] = np.asarray(res.results[core]["out"], np.float32)
    return full


# version 13 (final)
# speedup vs baseline: 1.0241x; 1.0241x over previous
import math
import numpy as np
from contextlib import ExitStack
import concourse.bass as bass
import concourse.mybir as mybir
from concourse.bass_utils import run_bass_kernel_spmd

F32 = mybir.dt.float32
BF16 = mybir.dt.bfloat16
AF = mybir.ActivationFunctionType
ALU = mybir.AluOpType
AX = mybir.AxisListType

ENGINES = ("sync", "gpsimd", "vector", "scalar", "tensor")
ESEM = {"sync": "dq", "gpsimd": "sg", "vector": "sv", "scalar": "sa", "tensor": "sp"}

EPS = 1e-5
QEPS = 1e-5
P23 = float(1.5 * 2.0 ** 23)
P52 = float(2.0 ** 52)
P52M = float(1.5 * 2.0 ** 52)


class Rec:

    def __init__(self):
        self.stream = {e: [] for e in ENGINES}
        self.count = {}

    def val(self, sem):
        return self.count.get(sem, 0)

    def emit(self, eng, method=None, kwargs=None, waits=(), inc=None):
        incs = []
        if method is not None:
            if inc is None:
                if method == "dma_start":
                    sem = "dg" if eng == "gpsimd" else "dq"
                    incs = [(sem, 16)]
                else:
                    incs = [(ESEM[eng], 1)]
            elif inc is False:
                incs = []
            else:
                incs = [inc]
        for s, a in incs:
            self.count[s] = self.count.get(s, 0) + a
        self.stream[eng].append((tuple(waits), method, kwargs or {}, incs))
        return self.val(incs[0][0]) if incs else None

    def replay(self, eng_name, eng, sems):
        for waits, method, kwargs, incs in self.stream[eng_name]:
            for s, v in waits:
                if v > 0:
                    eng.wait_ge(sems[s], v)
            if method is None:
                continue
            inst = getattr(eng, method)(**kwargs)
            for s, a in incs:
                inst = inst.then_inc(sems[s], a)


def build_program(D=2048, HEADS=16, T=2048, I=8192, G=512):
    TP = T
    HD = 128
    C = 128
    FC = D // 128
    NT = T // 128
    NP = TP // 128
    IC = I // 128
    NSUP = T // G
    GT = G // 128
    SCALE = float(HD) ** -0.5
    FWC = float(I) ** -0.5
    gammas = [float(g) for g in
              (1.0 - np.exp(np.linspace(math.log(1 / 32), math.log(1 / 512), HEADS)))]

    nc = bass.Bass("TRN2", target_bir_lowering=False)

    x_own = nc.dram_tensor("x_own", [T, D], F32, kind="ExternalInput").ap()
    x_pref = nc.dram_tensor("x_pref", [TP, D], F32, kind="ExternalInput").ap()
    maskin = nc.dram_tensor("maskin", [1, 1], F32, kind="ExternalInput").ap()
    w_attn = {}
    for nm in ("wq", "wk", "wv", "wg", "wo"):
        w_attn[nm] = nc.dram_tensor(nm + "_bc", [FC, D, 128], F32, kind="ExternalInput").ap()
    wgate_bc = nc.dram_tensor("wgate_bc", [IC, D, 128], F32, kind="ExternalInput").ap()
    wup_bc = nc.dram_tensor("wup_bc", [IC, D, 128], F32, kind="ExternalInput").ap()
    wdown_bc = nc.dram_tensor("wdown_bc", [FC, I, 128], F32, kind="ExternalInput").ap()
    ln_w = nc.dram_tensor("ln_w", [1, D], F32, kind="ExternalInput").ap()
    ln_b = nc.dram_tensor("ln_b", [1, D], F32, kind="ExternalInput").ap()
    out = nc.dram_tensor("out", [T, D], F32, kind="ExternalOutput").ap()
    x2_s = nc.dram_tensor("x2_s", [T, D], F32, kind="Internal").ap()
    yln_s = nc.dram_tensor("yln_s", [T, D], BF16, kind="Internal").ap()
    terng_s = nc.dram_tensor("terng_s", [IC, D, 128], BF16, kind="Internal").ap()
    ternu_s = nc.dram_tensor("ternu_s", [IC, D, 128], BF16, kind="Internal").ap()
    ternd_s = nc.dram_tensor("ternd_s", [FC, I, 128], BF16, kind="Internal").ap()
    dfm_s = nc.dram_tensor("dfm_s", [FC, GT, 128, 128], BF16, kind="Internal").ap()
    bounce = nc.dram_tensor("bounce", [1, 1024], F32, kind="Internal").ap()

    es = ExitStack()

    def sbt(name, shape, dt=F32):
        return es.enter_context(nc.sbuf_tensor(name, shape, dt))

    r = Rec()

    def barrier():
        snap = {s: r.val(s) for s in ("dq", "dg", "dg2", "dq2", "dqa", "dqb", "dq3", "dq4", "sv", "sa", "sg", "sp")}
        for e in ENGINES:
            w = [(s, v) for s, v in snap.items() if v > 0]
            r.emit(e, waits=w)

    def wall(*extra):
        base = [(s, r.val(s)) for s in ("sv", "sa", "sp", "sg")]
        return base + list(extra)

    with es:
        A1 = sbt("A1", [128, FC * T], BF16)
        A2 = sbt("A2", [128, FC * T], BF16)
        xt = sbt("xt", [128, D])
        ta = sbt("ta", [128, D])
        Sst = sbt("Sst", [128, HEADS * 128])
        qh = sbt("qh", [128, T], BF16)
        kh = sbt("kh", [128, T], BF16)
        vh = sbt("vh", [128, T], BF16)
        wsb = sbt("wsb", [128, 8192], BF16)
        fbuf = sbt("fbuf", [128, 4 * G])
        sb512 = sbt("sb512", [128, 2 * G], BF16)
        cols = sbt("cols", [128, 640])
        consts = sbt("consts", [128, 256], BF16)
        onesr = sbt("onesr", [1, 128])
        onesb = sbt("onesb", [1, 128], BF16)
        rowbuf = sbt("rowbuf", [1, 2048])
        rowsb = sbt("rowsb", [1, 512], BF16)
        mh = sbt("mh", [128, 128])
        rh = sbt("rh", [128, 128])
        sbf16 = sbt("sbf16", [128, 128], BF16)
        colb = sbt("colb", [128, 4], BF16)
        ktmb = sbt("ktmb", [128, 256], BF16)
        vtmb = sbt("vtmb", [128, 256], BF16)
        qcb = sbt("qcb", [128, 256], BF16)
        ssb = sbt("ssb", [128, 256], BF16)
        hf = sbt("hf", [128, 128])
        it32 = sbt("it32", [128, 1], mybir.dt.int32)
        PA = es.enter_context(nc.psum_tensor("PA", [128, 2048], F32))
        PB = es.enter_context(nc.psum_tensor("PB", [128, 1024], F32))
        PT = es.enter_context(nc.psum_tensor("PT", [128, 1024], BF16))
        PS = es.enter_context(nc.psum_tensor("PS", [128, 512], F32))

        semnames = ("dq", "dg", "dg2", "dq2", "dqa", "dqb", "dq3", "dq4", "dst", "sv", "sa", "sg", "sp")
        sems = {s: es.enter_context(nc.semaphore(s)) for s in semnames}
        block = es.enter_context(nc.Block())

        CO_P, CO_SEL, CO_A, CO_MASK, CO_GN, CO_TMP, CO_MEAN, CO_SQ, CO_RSTD = range(9)
        CO_KD = 16
        CO_KDP = 32
        CO_WS = 288
        CO_MW = 292
        CO_SC = 300
        CO_OS = 316
        CO_SCD = 332
        CO_OSD = 340
        CO_ACC = 352

        def c_(i):
            return cols[:, i:i + 1]

        def row_from_col(col_f32, dst_off):
            vb = r.emit("vector", "tensor_copy", dict(out=colb[:, 0:1], in_=col_f32),
                        waits=wall())
            tb = r.emit("tensor", "transpose",
                        dict(out=PT[0:1, 896:1024], in_=colb[:, 0:1], identity=consts[:, 0:128]),
                        waits=[("sv", vb), ("sa", r.val("sa"))])
            sa1 = r.emit("scalar", "copy",
                         dict(out=rowsb[0:1, dst_off:dst_off + 128], in_=PT[0:1, 896:1024]),
                         waits=[("sp", tb)])
            return sa1

        def bcast_row(row_off, dst_f32, n=128):
            tb = r.emit("tensor", "matmul",
                        dict(out=PS[:, 384:384 + n], lhsT=onesb[:, :],
                             rhs=rowsb[0:1, row_off:row_off + n], start=True, stop=True),
                        waits=[("sa", r.val("sa")), ("sv", r.val("sv"))])
            return r.emit("vector", "tensor_copy", dict(out=dst_f32, in_=PS[:, 384:384 + n]),
                          waits=[("sp", tb)])

        def ln_stats(src_ap, n_feat, scratch=None):
            scr = ta[:, 0:n_feat] if scratch is None else scratch
            a1 = r.emit("scalar", "activation",
                        dict(out=scr, in_=src_ap, func=AF.Square,
                             accum_out=c_(CO_SQ)),
                        waits=[("sv", r.val("sv")), ("sa", r.val("sa")),
                               ("dq", r.val("dq"))])
            v1 = r.emit("vector", "tensor_reduce",
                        dict(out=c_(CO_MEAN), in_=src_ap, axis=AX.X, op=ALU.add),
                        waits=[("sv", r.val("sv"))])
            v1 = r.emit("vector", "tensor_scalar",
                        dict(out=c_(CO_MEAN), in0=c_(CO_MEAN), scalar1=float(1.0 / n_feat),
                             scalar2=None, op0=ALU.mult), waits=[("sv", v1)])
            v2 = r.emit("vector", "tensor_tensor",
                        dict(out=c_(CO_TMP), in0=c_(CO_MEAN), in1=c_(CO_MEAN), op=ALU.mult),
                        waits=[("sv", v1)])
            v2 = r.emit("vector", "scalar_tensor_tensor",
                        dict(out=c_(CO_TMP), in0=c_(CO_SQ), scalar=float(1.0 / n_feat),
                             in1=c_(CO_TMP), op0=ALU.mult, op1=ALU.subtract),
                        waits=[("sv", v2), ("sa", a1)])
            v2 = r.emit("vector", "tensor_scalar",
                        dict(out=c_(CO_TMP), in0=c_(CO_TMP), scalar1=float(EPS), scalar2=None,
                             op0=ALU.add), waits=[("sv", v2)])
            a2 = r.emit("scalar", "activation",
                        dict(out=c_(CO_RSTD), in_=c_(CO_TMP), func=AF.Sqrt),
                        waits=[("sv", v2)])
            v3 = r.emit("vector", "reciprocal", dict(out=c_(CO_RSTD), in_=c_(CO_RSTD)),
                        waits=[("sa", a2)])
            return v3

        def transpose_tile(dst_view, src_tile, first_waits):
            for g0 in range(0, FC, 8):
                gn = min(8, FC - g0)
                last = None
                for j in range(gn):
                    w = (list(first_waits) + [("sa", r.val("sa"))]) if (g0 == 0 and j == 0) \
                        else ([("sa", r.val("sa"))] if j == 0 else [])
                    last = r.emit("tensor", "transpose",
                                  dict(out=PT[:, j * 128:(j + 1) * 128],
                                       in_=src_tile[:, (g0 + j) * 128:(g0 + j + 1) * 128],
                                       identity=consts[:, 0:128]),
                                  waits=w, inc=(("sp", 1) if j == gn - 1 else False))
                for j in range(gn):
                    r.emit("scalar", "copy",
                           dict(out=dst_view(g0 + j), in_=PT[:, j * 128:(j + 1) * 128]),
                           waits=([("sp", last)] if j == 0 else []))
            return r.val("sa")

        r.emit("vector", "memset", dict(ap=onesr[:, :], constant=1.0))
        v0 = r.emit("vector", "memset", dict(ap=onesb[:, :], constant=1.0))
        r.emit("gpsimd", "memset", dict(ap=consts[:, 0:128], constant=0.0))
        gi = r.emit("gpsimd", "affine_select",
                    dict(out=consts[:, 0:128], in_=consts[:, 0:128],
                         compare_op=ALU.not_equal, fill=1.0, base=0,
                         pattern=[[-1, 128]], channel_multiplier=1),
                    waits=[("sg", r.val("sg"))])
        gio = r.emit("gpsimd", "iota",
                     dict(out=it32[:, :], pattern=[[0, 1]], base=0, channel_multiplier=1))
        vP = r.emit("vector", "tensor_copy", dict(out=c_(CO_P), in_=it32[:, :]),
                    waits=[("sg", gio)])
        v1 = r.emit("vector", "memset", dict(ap=c_(CO_SEL), constant=1.0), waits=[("sv", vP)])
        v1 = r.emit("vector", "tensor_scalar",
                    dict(out=c_(CO_SEL), in0=c_(CO_SEL), scalar1=P52, scalar2=None, op0=ALU.add),
                    waits=[("sv", v1)])
        v1 = r.emit("vector", "tensor_scalar",
                    dict(out=c_(CO_SEL), in0=c_(CO_SEL), scalar1=P52, scalar2=None,
                         op0=ALU.subtract), waits=[("sv", v1)])
        vA = r.emit("vector", "tensor_scalar",
                    dict(out=c_(CO_A), in0=c_(CO_SEL), scalar1=P52M, scalar2=P23,
                         op0=ALU.mult, op1=ALU.add), waits=[("sv", v1)])
        dmask = r.emit("sync", "dma_start", dict(out=rowbuf[0:1, 0:1], in_=maskin))
        tmask = r.emit("tensor", "matmul",
                       dict(out=PS[:, 0:1], lhsT=onesr[:, :], rhs=rowbuf[0:1, 0:1],
                            start=True, stop=True),
                       waits=[("dq", dmask), ("sv", r.val("sv"))])
        vM = r.emit("vector", "tensor_copy", dict(out=c_(CO_MASK), in_=PS[:, 0:1]),
                    waits=[("sp", tmask)])
        sa_kd = []
        for h in range(HEADS):
            lg = math.log(gammas[h])
            vpre = r.emit("vector", "tensor_scalar",
                          dict(out=c_(CO_KD + h), in0=c_(CO_P), scalar1=float(-lg),
                               scalar2=float(C * lg), op0=ALU.mult, op1=ALU.add),
                          waits=[("sv", vP)])
            a = r.emit("scalar", "activation",
                       dict(out=c_(CO_KD + h), in_=c_(CO_KD + h), func=AF.Exp),
                       waits=[("sv", vpre)])
            sa_kd.append(a)
        for h in range(HEADS):
            for cc in range(NP):
                age = gammas[h] ** ((NP - 1 - cc) * C)
                r.emit("vector", "tensor_scalar",
                       dict(out=c_(CO_KDP + h * NP + cc), in0=c_(CO_KD + h),
                            scalar1=float(age), scalar2=None, op0=ALU.mult),
                       waits=[("sa", sa_kd[h])])
        vH = r.emit("vector", "memset", dict(ap=hf[:, :], constant=1.0),
                    waits=[("sv", r.val("sv"))])
        for k in range(7):
            v2 = r.emit("vector", "tensor_scalar",
                        dict(out=c_(CO_TMP), in0=c_(CO_P), scalar1=float(2.0 ** -k),
                             scalar2=float(-0.5 + 2.0 ** -(k + 1)), op0=ALU.mult, op1=ALU.add),
                        waits=[("sv", vA), ("sv", vH)])
            v2 = r.emit("vector", "tensor_scalar",
                        dict(out=c_(CO_TMP), in0=c_(CO_TMP), scalar1=c_(CO_A), scalar2=None,
                             op0=ALU.add), waits=[("sv", v2)])
            v2 = r.emit("vector", "tensor_scalar",
                        dict(out=c_(CO_TMP), in0=c_(CO_TMP), scalar1=c_(CO_A), scalar2=None,
                             op0=ALU.subtract), waits=[("sv", v2)])
            v3 = r.emit("vector", "tensor_scalar",
                        dict(out=c_(CO_GN), in0=c_(CO_TMP), scalar1=0.5, scalar2=-0.25,
                             op0=ALU.mult, op1=ALU.add), waits=[("sv", v2)])
            v3 = r.emit("vector", "tensor_scalar",
                        dict(out=c_(CO_GN), in0=c_(CO_GN), scalar1=c_(CO_A), scalar2=None,
                             op0=ALU.add), waits=[("sv", v3)])
            v3 = r.emit("vector", "tensor_scalar",
                        dict(out=c_(CO_GN), in0=c_(CO_GN), scalar1=c_(CO_A), scalar2=None,
                             op0=ALU.subtract), waits=[("sv", v3)])
            v4 = r.emit("vector", "scalar_tensor_tensor",
                        dict(out=c_(CO_GN), in0=c_(CO_GN), scalar=-2.0, in1=c_(CO_TMP),
                             op0=ALU.mult, op1=ALU.add), waits=[("sv", v3)])
            sa1 = row_from_col(c_(CO_GN), 0)
            to = r.emit("tensor", "matmul",
                        dict(out=PS[:, 0:128], lhsT=rowsb[0:1, 0:128], rhs=rowsb[0:1, 0:128],
                             start=True, stop=True),
                        waits=[("sa", sa1), ("sv", r.val("sv"))])
            v5 = r.emit("vector", "tensor_scalar",
                        dict(out=ta[:, 0:128], in0=PS[:, 0:128], scalar1=-2.0, scalar2=1.0,
                             op0=ALU.mult, op1=ALU.add), waits=[("sp", to)])
            vH = r.emit("vector", "tensor_tensor",
                        dict(out=hf[:, :], in0=hf[:, :], in1=ta[:, 0:128], op=ALU.mult),
                        waits=[("sv", v5)])
        vHp = r.emit("vector", "tensor_copy", dict(out=consts[:, 128:256], in_=hf[:, :]),
                     waits=[("sv", vH)])
        lblks = [(i, min(512, D - i)) for i in range(0, D, 512)]
        dln = r.emit("sync", "dma_start", dict(out=rowbuf[0:1, 0:D], in_=ln_w),
                     waits=[("sp", tmask)])
        for bi, (b0, bw) in enumerate(lblks):
            r.emit("tensor", "matmul",
                   dict(out=PA[:, b0:b0 + bw], lhsT=onesr[:, :],
                        rhs=rowbuf[0:1, b0:b0 + bw], start=True, stop=True),
                   waits=([("dq", dln), ("sv", r.val("sv"))] if bi == 0 else []),
                   inc=(("sp", 1) if bi == len(lblks) - 1 else False))
        va = r.emit("vector", "tensor_copy", dict(out=wsb[:, 4096:4096 + D], in_=PA[:, 0:D]),
                    waits=[("sp", r.val("sp"))])
        dlb = r.emit("sync", "dma_start", dict(out=rowbuf[0:1, 0:D], in_=ln_b),
                     waits=[("sv", va)])
        for bi, (b0, bw) in enumerate(lblks):
            r.emit("tensor", "matmul",
                   dict(out=PA[:, b0:b0 + bw], lhsT=onesr[:, :],
                        rhs=rowbuf[0:1, b0:b0 + bw], start=True, stop=True),
                   waits=([("dq", dlb), ("sv", va)] if bi == 0 else []),
                   inc=(("sp", 1) if bi == len(lblks) - 1 else False))
        r.emit("vector", "tensor_copy", dict(out=wsb[:, 4096 + D:4096 + 2 * D], in_=PA[:, 0:D]),
               waits=[("sp", r.val("sp"))])
        barrier()

        def ln1_phase(src_dram, TT):
            ntile = TT // 128
            for t in range(ntile):
                d1 = r.emit("sync", "dma_start",
                            dict(out=xt[:, :], in_=src_dram[t * 128:(t + 1) * 128, :]),
                            waits=wall(("dq", r.val("dq"))))
                r.emit("vector", waits=[("dq", d1)])
                r.emit("scalar", waits=[("dq", d1)])
                vS = ln_stats(xt[:, :], D)
                vap = r.emit("vector", "tensor_scalar",
                             dict(out=wsb[:, 0:D], in0=xt[:, :], scalar1=c_(CO_MEAN),
                                  scalar2=c_(CO_RSTD), op0=ALU.subtract, op1=ALU.mult),
                             waits=[("sv", vS)])
                transpose_tile(
                    lambda k, _t=t: A1[:, k * TT + _t * 128: k * TT + (_t + 1) * 128],
                    wsb[:, 0:D], [("sv", vap)])
            barrier()

        ln1_phase(x_pref, TP)

        def load_wcol(dst_slice, src_bc, idx, waits=()):
            return r.emit("gpsimd", "dma_start",
                          dict(out=dst_slice.rearrange("p (k c) -> p k c", c=128),
                               in_=src_bc[idx].rearrange("(k p) c -> p k c", p=128)),
                          waits=waits)

        def proj_head(wslot, TT, dg_wait, dsem="dg"):
            blks = [(i, min(512, TT - i)) for i in range(0, TT, 512)]
            for bi, (b0, bw) in enumerate(blks):
                for k in range(FC):
                    r.emit("tensor", "matmul",
                           dict(out=PA[:, b0:b0 + bw],
                                lhsT=wsb[:, wslot * D + k * 128: wslot * D + (k + 1) * 128],
                                rhs=A1[:, k * TT + b0: k * TT + b0 + bw],
                                start=(k == 0), stop=(k == FC - 1)),
                           waits=([(dsem, dg_wait), ("sa", r.val("sa")), ("sv", r.val("sv"))]
                                  if (bi == 0 and k == 0) else []),
                           inc=(("sp", 1) if (k == FC - 1 and bi == len(blks) - 1) else False))
            return r.val("sp")

        def proj_head_v(wslot, TT, dg_wait):
            return proj_head(wslot, TT, dg_wait, dsem="dg2")

        sp_projp = {}
        for h in range(HEADS):
            wkd = r.emit("gpsimd", "dma_start",
                         dict(out=wsb[:, 0:D].rearrange("p (k c) -> p k c", c=128),
                              in_=w_attn["wk"][h].rearrange("(k p) c -> p k c", p=128)),
                         waits=[("sp", sp_projp.get(h - 1, 0))], inc=("dg", 16))
            wvd = r.emit("gpsimd", "dma_start",
                         dict(out=wsb[:, D:2 * D].rearrange("p (k c) -> p k c", c=128),
                              in_=w_attn["wv"][h].rearrange("(k p) c -> p k c", p=128)),
                         inc=("dg2", 16))
            spk = proj_head(0, TP, r.val("dg"))
            sak = r.emit("scalar", "copy", dict(out=kh[:, 0:TP], in_=PA[:, 0:TP]),
                         waits=[("sp", spk)])
            spv = proj_head_v(1, TP, r.val("dg2"))
            sp_projp[h] = spv
            sav = r.emit("scalar", "copy", dict(out=vh[:, 0:TP], in_=PA[:, 0:TP]),
                         waits=[("sp", spv)])
            pvv_of = {}
            for cc in range(NP):
                cs = slice(cc * 128, (cc + 1) * 128)
                pr = cc % 2
                PTa = PT[:, 0:128] if pr == 0 else PT[:, 256:384]
                PTb = PT[:, 128:256] if pr == 0 else PT[:, 384:512]
                kt = ktmb[:, pr * 128:(pr + 1) * 128]
                vt = vtmb[:, pr * 128:(pr + 1) * 128]
                tk1 = r.emit("tensor", "transpose",
                             dict(out=PTa, in_=kh[:, cs], identity=consts[:, 0:128]),
                             waits=[("sa", sav), ("sv", pvv_of.get(cc - 2, 0))])
                tk2 = r.emit("tensor", "transpose",
                             dict(out=PTb, in_=vh[:, cs], identity=consts[:, 0:128]))
                vk = r.emit("vector", "tensor_scalar",
                            dict(out=kt, in0=PTa,
                                 scalar1=c_(CO_KDP + h * NP + cc), scalar2=None, op0=ALU.mult),
                            waits=[("sp", tk2)])
                vv = r.emit("vector", "tensor_copy", dict(out=vt, in_=PTb),
                            waits=[("sv", vk)])
                pvv_of[cc] = vv
                r.emit("tensor", "matmul",
                       dict(out=PS[:, 0:128], lhsT=kt, rhs=vt,
                            start=(cc == 0), stop=(cc == NP - 1), skip_group_check=True),
                       waits=[("sv", vv)],
                       inc=(("sp", 1) if cc == NP - 1 else False))
            spS = r.val("sp")
            r.emit("vector", "tensor_scalar",
                   dict(out=Sst[:, h * 128:(h + 1) * 128], in0=PS[:, 0:128],
                        scalar1=c_(CO_MASK), scalar2=None, op0=ALU.mult),
                   waits=[("sp", spS), ("sv", vM)])
        barrier()

        ln1_phase(x_own, T)

        sp_mix = {}
        sp_v = {}
        for h in range(HEADS):
            lg = math.log(gammas[h])
            vpre = r.emit("vector", "tensor_scalar",
                          dict(out=c_(CO_GN), in0=c_(CO_P), scalar1=float(lg),
                               scalar2=float(math.log(SCALE)), op0=ALU.mult, op1=ALU.add),
                          waits=wall())
            aR = r.emit("scalar", "activation",
                        dict(out=c_(CO_GN), in_=c_(CO_GN), func=AF.Exp),
                        waits=[("sv", vpre)])
            saR = row_from_col(c_(CO_GN), 0)
            bcast_row(0, rh[:, :])
            vpre = r.emit("vector", "tensor_scalar",
                          dict(out=c_(CO_GN), in0=c_(CO_P), scalar1=float(-lg),
                               scalar2=None, op0=ALU.mult),
                          waits=[("sa", saR), ("sv", r.val("sv"))])
            aG = r.emit("scalar", "activation",
                        dict(out=c_(CO_GN), in_=c_(CO_GN), func=AF.Exp),
                        waits=[("sv", vpre)])
            saG = row_from_col(c_(CO_GN), 128)
            tM = r.emit("tensor", "matmul",
                        dict(out=PS[:, 0:128], lhsT=rowsb[0:1, 128:256],
                             rhs=rowsb[0:1, 0:128], start=True, stop=True),
                        waits=[("sa", saG), ("sv", r.val("sv"))])
            vMh = r.emit("vector", "tensor_copy", dict(out=mh[:, :], in_=PS[:, 0:128]),
                         waits=[("sp", tM)])
            gMh = r.emit("gpsimd", "affine_select",
                         dict(out=mh[:, :], in_=mh[:, :], compare_op=ALU.is_ge, fill=0.0,
                              base=0, pattern=[[1, 128]], channel_multiplier=-1),
                         waits=[("sv", vMh), ("sg", r.val("sg"))])
            r.emit("vector", waits=[("sg", gMh)])
            sa0 = r.emit("scalar", "copy",
                         dict(out=sbf16[:, :], in_=Sst[:, h * 128:(h + 1) * 128]),
                         waits=[("sv", r.val("sv"))])
            wqd = r.emit("gpsimd", "dma_start",
                         dict(out=wsb[:, 0:D].rearrange("p (k c) -> p k c", c=128),
                              in_=w_attn["wq"][h].rearrange("(k p) c -> p k c", p=128)),
                         waits=[("sp", sp_v.get(h - 1, 0))], inc=("dg", 16))
            wkd = r.emit("gpsimd", "dma_start",
                         dict(out=wsb[:, D:2 * D].rearrange("p (k c) -> p k c", c=128),
                              in_=w_attn["wk"][h].rearrange("(k p) c -> p k c", p=128)),
                         inc=("dg2", 16))
            spq = proj_head(0, T, r.val("dg"))
            saq = r.emit("scalar", "copy", dict(out=qh[:, :], in_=PA[:, 0:T]),
                         waits=[("sp", spq)])
            wvd = r.emit("gpsimd", "dma_start",
                         dict(out=wsb[:, 0:D].rearrange("p (k c) -> p k c", c=128),
                              in_=w_attn["wv"][h].rearrange("(k p) c -> p k c", p=128)),
                         waits=[("sp", spq)], inc=("dg", 16))
            spk = proj_head_v(1, T, r.val("dg2"))
            sak = r.emit("scalar", "copy", dict(out=kh[:, :], in_=PA[:, 0:T]),
                         waits=[("sp", spk)])
            spv = proj_head(0, T, r.val("dg"))
            sp_v[h] = spv
            sav = r.emit("scalar", "copy", dict(out=vh[:, :], in_=PA[:, 0:T]),
                         waits=[("sp", spv)])
            t4_of, t5_of, t6_of = {}, {}, {}
            say_of, sa0_of, vS1_of, vk_of, vS2_of = {}, {-1: sa0}, {}, {}, {}
            for cc in range(NT):
                cs = slice(cc * 128, (cc + 1) * 128)
                pr = cc % 2
                SC_PS = PS[:, 0:128] if pr == 0 else PA[:, 0:128]
                O_PS = PB[:, 0:128] if pr == 0 else PA[:, 512:640]
                DS_PS = PB[:, 512:640] if pr == 0 else PA[:, 1024:1152]
                PTa = PT[:, 0:128] if pr == 0 else PT[:, 256:384]
                PTb = PT[:, 128:256] if pr == 0 else PT[:, 384:512]
                kt = ktmb[:, pr * 128:(pr + 1) * 128]
                vt = vtmb[:, pr * 128:(pr + 1) * 128]
                sb_ = ssb[:, pr * 128:(pr + 1) * 128]
                qc = qcb[:, pr * 128:(pr + 1) * 128]
                t1 = r.emit("tensor", "matmul",
                            dict(out=SC_PS, lhsT=kh[:, cs], rhs=qh[:, cs],
                                 start=True, stop=True, skip_group_check=True),
                            waits=[("sa", sav), ("sv", vS1_of.get(cc - 2, 0))])
                t2 = r.emit("tensor", "transpose",
                            dict(out=PTa, in_=kh[:, cs], identity=consts[:, 0:128]),
                            waits=[("sv", vk_of.get(cc - 2, 0))])
                t3 = r.emit("tensor", "transpose",
                            dict(out=PTb, in_=vh[:, cs], identity=consts[:, 0:128]))
                vS1 = r.emit("vector", "tensor_tensor",
                             dict(out=sb_, in0=SC_PS, in1=mh[:, :], op=ALU.mult),
                             waits=[("sp", t1)])
                vS1_of[cc] = vS1
                vq = r.emit("vector", "tensor_tensor",
                            dict(out=qc, in0=qh[:, cs], in1=rh[:, :], op=ALU.mult),
                            waits=[("sv", vS1), ("sp", t5_of.get(cc - 2, 0))])
                vk = r.emit("vector", "tensor_scalar",
                            dict(out=kt, in0=PTa, scalar1=c_(CO_KD + h),
                                 scalar2=None, op0=ALU.mult),
                            waits=[("sp", t3), ("sv", vq)])
                vk_of[cc] = vk
                vv = r.emit("vector", "tensor_copy", dict(out=vt, in_=PTb),
                            waits=[("sv", vk)])
                t4 = r.emit("tensor", "matmul",
                            dict(out=O_PS, lhsT=sb_, rhs=vt,
                                 start=True, stop=False, skip_group_check=True),
                            waits=[("sv", vv), ("sa", say_of.get(cc - 2, 0))])
                t4_of[cc] = t4
                t5 = r.emit("tensor", "matmul",
                            dict(out=O_PS, lhsT=qc, rhs=sbf16[:, :],
                                 start=False, stop=True, skip_group_check=True),
                            waits=[("sa", sa0_of[cc - 1])])
                t5_of[cc] = t5
                say = r.emit("scalar", "copy",
                             dict(out=A2[:, cc * D + h * 128: cc * D + (h + 1) * 128],
                                  in_=O_PS), waits=[("sp", t5)])
                say_of[cc] = say
                t6 = r.emit("tensor", "matmul",
                            dict(out=DS_PS, lhsT=kt, rhs=vt,
                                 start=True, stop=True, skip_group_check=True),
                            waits=[("sv", vS2_of.get(cc - 2, 0))])
                t6_of[cc] = t6
                vS2 = r.emit("vector", "scalar_tensor_tensor",
                             dict(out=Sst[:, h * 128:(h + 1) * 128],
                                  in0=Sst[:, h * 128:(h + 1) * 128],
                                  scalar=float(gammas[h] ** C), in1=DS_PS,
                                  op0=ALU.mult, op1=ALU.add),
                             waits=[("sp", t6)])
                vS2_of[cc] = vS2
                sa0 = r.emit("scalar", "copy",
                             dict(out=sbf16[:, :], in_=Sst[:, h * 128:(h + 1) * 128]),
                             waits=[("sv", vS2), ("sp", t5)])
                sa0_of[cc] = sa0
            sp_mix[h] = r.val("sp")
        barrier()

        for t in range(NT):
            src = A2[:, t * D:(t + 1) * D]
            vS = ln_stats(src, D)
            v1 = r.emit("vector", "tensor_scalar",
                        dict(out=ta[:, :], in0=src, scalar1=c_(CO_MEAN), scalar2=c_(CO_RSTD),
                             op0=ALU.subtract, op1=ALU.mult), waits=[("sv", vS)])
            v1 = r.emit("vector", "tensor_tensor",
                        dict(out=ta[:, :], in0=ta[:, :], in1=wsb[:, 4096:4096 + D], op=ALU.mult),
                        waits=[("sv", v1)])
            v1 = r.emit("vector", "tensor_tensor",
                        dict(out=src, in0=ta[:, :], in1=wsb[:, 4096 + D:4096 + 2 * D], op=ALU.add),
                        waits=[("sv", v1)])
            r.emit("sync", "dma_start",
                   dict(out=yln_s[t * 128:(t + 1) * 128, :], in_=src),
                   waits=[("sv", v1)])
        barrier()

        for j in range(FC):
            wgd = load_wcol(wsb[:, (j % 2) * D:(j % 2) * D + D], w_attn["wg"], j,
                            waits=[("sp", r.val("sp"))])
            blks = [(i, min(512, T - i)) for i in range(0, T, 512)]
            for bi, (b0, bw) in enumerate(blks):
                for k in range(FC):
                    r.emit("tensor", "matmul",
                           dict(out=PA[:, b0:b0 + bw],
                                lhsT=wsb[:, (j % 2) * D + k * 128:(j % 2) * D + (k + 1) * 128],
                                rhs=A1[:, k * T + b0: k * T + b0 + bw],
                                start=(k == 0), stop=(k == FC - 1)),
                           waits=([("dg", wgd), ("sa", r.val("sa")), ("sv", r.val("sv"))]
                                  if (bi == 0 and k == 0) else []),
                           inc=(("sp", 1) if (k == FC - 1 and bi == len(blks) - 1) else False))
            r.emit("scalar", "activation",
                   dict(out=A2[:, j * T:(j + 1) * T], in_=PA[:, 0:T], func=AF.Silu),
                   waits=[("sp", r.val("sp"))])
        barrier()
        for t in range(NT):
            r.emit("sync", "dma_start",
                   dict(out=A1[:, t * D:(t + 1) * D], in_=yln_s[t * 128:(t + 1) * 128, :]),
                   waits=([("dq", r.val("dq"))] if t == 0 else []))
        barrier()
        for j in range(FC):
            for g0 in range(0, NT, 8):
                gn = min(8, NT - g0)
                last = None
                for tt_ in range(gn):
                    t = g0 + tt_
                    last = r.emit("tensor", "transpose",
                                  dict(out=PT[:, tt_ * 128:(tt_ + 1) * 128],
                                       in_=A1[:, t * D + j * 128: t * D + (j + 1) * 128],
                                       identity=consts[:, 0:128]),
                                  waits=([("sv", r.val("sv"))] if tt_ == 0 else []),
                                  inc=(("sp", 1) if tt_ == gn - 1 else False))
                r.emit("vector", "tensor_tensor",
                       dict(out=A2[:, j * T + g0 * 128: j * T + (g0 + gn) * 128],
                            in0=PT[:, 0:gn * 128],
                            in1=A2[:, j * T + g0 * 128: j * T + (g0 + gn) * 128],
                            op=ALU.mult),
                       waits=[("sp", last)])
        barrier()

        for j in range(FC):
            wod = load_wcol(wsb[:, (j % 2) * D:(j % 2) * D + D], w_attn["wo"], j,
                            waits=[("sp", r.val("sp"))])
            blks = [(i, min(512, T - i)) for i in range(0, T, 512)]
            for bi, (b0, bw) in enumerate(blks):
                for k in range(FC):
                    r.emit("tensor", "matmul",
                           dict(out=PA[:, b0:b0 + bw],
                                lhsT=wsb[:, (j % 2) * D + k * 128:(j % 2) * D + (k + 1) * 128],
                                rhs=A2[:, k * T + b0: k * T + b0 + bw],
                                start=(k == 0), stop=(k == FC - 1)),
                           waits=([("dg", wod), ("sa", r.val("sa")), ("sv", r.val("sv"))]
                                  if (bi == 0 and k == 0) else []),
                           inc=(("sp", 1) if (k == FC - 1 and bi == len(blks) - 1) else False))
            r.emit("scalar", "copy",
                   dict(out=A1[:, j * T:(j + 1) * T], in_=PA[:, 0:T]),
                   waits=[("sp", r.val("sp"))])
        barrier()

        for t in range(NT):
            for g0 in range(0, FC, 8):
                gn = min(8, FC - g0)
                last = None
                for jj in range(gn):
                    j = g0 + jj
                    last = r.emit("tensor", "transpose",
                                  dict(out=PT[:, jj * 128:(jj + 1) * 128],
                                       in_=A1[:, j * T + t * 128: j * T + (t + 1) * 128],
                                       identity=consts[:, 0:128]),
                                  waits=([("sv", r.val("sv")), ("sa", r.val("sa"))]
                                         if jj == 0 else []),
                                  inc=(("sp", 1) if jj == gn - 1 else False))
                r.emit("scalar", "copy",
                       dict(out=ta[:, g0 * 128:(g0 + gn) * 128], in_=PT[:, 0:gn * 128]),
                       waits=[("sp", last)])
            r.emit("vector", waits=[("sa", r.val("sa"))])
            vS = ln_stats(ta[:, :], D, scratch=xt[:, :])
            v1 = r.emit("vector", "tensor_scalar",
                        dict(out=ta[:, :], in0=ta[:, :], scalar1=c_(CO_MEAN),
                             scalar2=c_(CO_RSTD), op0=ALU.subtract, op1=ALU.mult),
                        waits=[("sv", vS)])
            d1 = r.emit("sync", "dma_start",
                        dict(out=xt[:, :], in_=x_own[t * 128:(t + 1) * 128, :]),
                        waits=[("sv", r.val("sv")), ("dq", r.val("dq")),
                               ("dst", r.val("dst"))])
            v2 = r.emit("vector", "tensor_tensor",
                        dict(out=xt[:, :], in0=ta[:, :], in1=xt[:, :], op=ALU.add),
                        waits=[("sv", v1), ("dq", d1)])
            r.emit("sync", "dma_start",
                   dict(out=x2_s[t * 128:(t + 1) * 128, :], in_=xt[:, :]),
                   waits=[("sv", v2)])
            vS = ln_stats(xt[:, :], D)
            v3 = r.emit("vector", "tensor_scalar",
                        dict(out=ta[:, :], in0=xt[:, :], scalar1=c_(CO_MEAN),
                             scalar2=c_(CO_RSTD), op0=ALU.subtract, op1=ALU.mult),
                        waits=[("sv", vS)])
            v4 = r.emit("vector", "tensor_reduce",
                        dict(out=c_(CO_TMP), in_=ta[:, :], axis=AX.X, op=ALU.max,
                             apply_absolute_value=True), waits=[("sv", v3)])
            v4 = r.emit("vector", "tensor_scalar",
                        dict(out=c_(CO_OS + t), in0=c_(CO_TMP), scalar1=float(QEPS),
                             scalar2=float(1.0 / 128.0), op0=ALU.max, op1=ALU.mult),
                        waits=[("sv", v4)])
            v4 = r.emit("vector", "reciprocal", dict(out=c_(CO_SC + t), in_=c_(CO_OS + t)),
                        waits=[("sv", v4)])
            v5 = r.emit("vector", "tensor_scalar",
                        dict(out=ta[:, :], in0=ta[:, :], scalar1=c_(CO_SC + t), scalar2=c_(CO_A),
                             op0=ALU.mult, op1=ALU.add), waits=[("sv", v4)])
            v5 = r.emit("vector", "tensor_scalar",
                        dict(out=ta[:, :], in0=ta[:, :], scalar1=c_(CO_A), scalar2=127.0,
                             op0=ALU.subtract, op1=ALU.min), waits=[("sv", v5)])
            v5 = r.emit("vector", "tensor_scalar",
                        dict(out=wsb[:, 0:D], in0=ta[:, :], scalar1=-128.0, scalar2=None,
                             op0=ALU.max), waits=[("sv", v5)])
            transpose_tile(
                lambda k, _t=t: A2[:, k * T + _t * 128: k * T + (_t + 1) * 128],
                wsb[:, 0:D], [("sv", v5)])
        barrier()

        wm_cons = {}

        def wmean_block(src, acc_idx, i, width=None):
            wd = D if width is None else width
            par = i % 2
            qs = "dq" if par == 0 else "dq2"
            stage = xt[:, 0:wd] if par == 0 else fbuf[:, 0:wd]
            d_ = r.emit("sync", "dma_start",
                        dict(out=stage.rearrange("p (k c) -> p k c", c=128), in_=src),
                        waits=([wm_cons[i - 2]] if (i - 2) in wm_cons else
                               [("sv", r.val("sv")), ("sa", r.val("sa"))]),
                        inc=(qs, 16))
            if par == 0:
                r.emit("vector", waits=[(qs, d_)])
                c2 = r.emit("vector", "tensor_reduce",
                            dict(out=c_(CO_ACC + acc_idx), in_=stage, axis=AX.X,
                                 op=ALU.add, apply_absolute_value=True))
                wm_cons[i] = ("sv", c2)
            else:
                r.emit("scalar", waits=[(qs, d_)])
                c2 = r.emit("scalar", "activation",
                            dict(out=ta[:, 0:wd], in_=stage, func=AF.Abs,
                                 accum_out=c_(CO_ACC + acc_idx)))
                wm_cons[i] = ("sa", c2)

        for i in range(IC):
            wmean_block(wgate_bc[i].rearrange("(k p) c -> p k c", p=128), i, i)
        for i in range(IC):
            wmean_block(wup_bc[i].rearrange("(k p) c -> p k c", p=128), IC + i, IC + i)
        for i in range(FC * 4):
            col, piece = i // 4, i % 4
            wmean_block(wdown_bc[col][piece * (I // 4):(piece + 1) * (I // 4)]
                        .rearrange("(k p) c -> p k c", p=128), 2 * IC + i, 2 * IC + i,
                        width=I // 4)
        barrier()
        nacc_g, nacc_u, nacc_d = IC, IC, FC * 4
        v6 = r.emit("vector", "tensor_reduce",
                    dict(out=c_(CO_TMP), in_=cols[:, CO_ACC:CO_ACC + nacc_g], axis=AX.X,
                         op=ALU.add))
        v6 = r.emit("vector", "tensor_reduce",
                    dict(out=cols[:, CO_TMP + 1:CO_TMP + 2],
                         in_=cols[:, CO_ACC + nacc_g:CO_ACC + nacc_g + nacc_u], axis=AX.X,
                         op=ALU.add), waits=[("sv", v6)])
        v6 = r.emit("vector", "tensor_reduce",
                    dict(out=cols[:, CO_TMP + 2:CO_TMP + 3],
                         in_=cols[:, CO_ACC + nacc_g + nacc_u:
                                  CO_ACC + nacc_g + nacc_u + nacc_d], axis=AX.X,
                         op=ALU.add), waits=[("sv", v6)])
        d7 = r.emit("sync", "dma_start",
                    dict(out=bounce[0:1, 0:384], in_=cols[:, CO_TMP:CO_TMP + 3]),
                    waits=[("sv", v6)])
        d8 = r.emit("sync", "dma_start",
                    dict(out=rowbuf[0:1, 0:384], in_=bounce[0:1, 0:384]),
                    waits=[("dq", d7)])
        r.emit("vector", waits=[("dq", d8)])
        vm = r.val("sv")
        for wi in range(3):
            vm = r.emit("vector", "tensor_reduce",
                        dict(out=rowbuf[0:1, 384 + wi:385 + wi],
                             in_=rowbuf[0:1, 0:384].rearrange("o (p c) -> o c p", c=3)[:, wi, :],
                             axis=AX.X, op=ALU.add),
                        waits=[("sv", vm)])
        numel = float(I * D)
        for wi in range(3):
            vm = r.emit("vector", "tensor_scalar",
                        dict(out=rowbuf[0:1, 388 + wi:389 + wi],
                             in0=rowbuf[0:1, 384 + wi:385 + wi],
                             scalar1=float(1.0 / numel), scalar2=float(QEPS),
                             op0=ALU.mult, op1=ALU.max), waits=[("sv", vm)])
        vm = r.emit("vector", "reciprocal",
                    dict(out=rowbuf[0:1, 392:395], in_=rowbuf[0:1, 388:391]),
                    waits=[("sv", vm)])
        t9 = r.emit("tensor", "matmul",
                    dict(out=PS[:, 0:3], lhsT=onesr[:, :], rhs=rowbuf[0:1, 388:391],
                         start=True, stop=True),
                    waits=[("sv", vm), ("sa", r.val("sa"))])
        vm2 = r.emit("vector", "tensor_copy", dict(out=cols[:, CO_MW:CO_MW + 3], in_=PS[:, 0:3]),
                     waits=[("sp", t9)])
        vh1 = r.emit("vector", "tensor_copy", dict(out=rowsb[0:1, 0:3], in_=rowbuf[0:1, 392:395]),
                     waits=[("sv", vm2)])
        vh2 = r.emit("vector", "tensor_tensor",
                     dict(out=rowbuf[0:1, 396:399], in0=rowbuf[0:1, 392:395],
                          in1=rowsb[0:1, 0:3], op=ALU.subtract), waits=[("sv", vh1)])
        vh3 = r.emit("vector", "tensor_copy", dict(out=rowsb[0:1, 4:7], in_=rowbuf[0:1, 396:399]),
                     waits=[("sv", vh2)])
        vh4 = r.emit("vector", "tensor_tensor",
                     dict(out=rowbuf[0:1, 396:399], in0=rowbuf[0:1, 396:399],
                          in1=rowsb[0:1, 4:7], op=ALU.subtract), waits=[("sv", vh3)])
        vh5 = r.emit("vector", "tensor_copy", dict(out=rowsb[0:1, 8:11], in_=rowbuf[0:1, 396:399]),
                     waits=[("sv", vh4)])
        tw1 = r.emit("tensor", "matmul",
                     dict(out=PS[:, 8:11], lhsT=onesb[:, :], rhs=rowsb[0:1, 0:3],
                          start=True, stop=False, skip_group_check=True),
                     waits=[("sv", vh5), ("sa", r.val("sa"))], inc=False)
        tw1 = r.emit("tensor", "matmul",
                     dict(out=PS[:, 8:11], lhsT=onesb[:, :], rhs=rowsb[0:1, 4:7],
                          start=False, stop=False, skip_group_check=True), inc=False)
        tw1 = r.emit("tensor", "matmul",
                     dict(out=PS[:, 8:11], lhsT=onesb[:, :], rhs=rowsb[0:1, 8:11],
                          start=False, stop=True, skip_group_check=True))
        r.emit("vector", "tensor_copy", dict(out=cols[:, CO_WS:CO_WS + 3], in_=PS[:, 8:11]),
               waits=[("sp", tw1)])
        barrier()

        tn_cons = {}
        tn_store = {}

        def tern_block(src, dst, ws_idx, i, width=None):
            wd = D if width is None else width
            par = i % 2
            qs = "dq" if par == 0 else "dq2"
            ss = "dq3" if par == 0 else "dq4"
            stage = xt[:, 0:wd] if par == 0 else fbuf[:, 0:wd]
            oslot = wsb[:, 0:wd] if par == 0 else wsb[:, 2048:2048 + wd]
            d_ = r.emit("sync", "dma_start",
                        dict(out=stage.rearrange("p (k c) -> p k c", c=128), in_=src),
                        waits=([tn_cons[i - 2]] if (i - 2) in tn_cons else
                               [("sv", r.val("sv")), ("sa", r.val("sa"))]),
                        inc=(qs, 16))
            v_ = r.emit("vector", "tensor_scalar",
                        dict(out=ta[:, 0:wd], in0=stage, scalar1=c_(CO_WS + ws_idx),
                             scalar2=c_(CO_A), op0=ALU.mult, op1=ALU.add),
                        waits=[(qs, d_), ("sv", r.val("sv"))])
            tn_cons[i] = ("sv", v_)
            v_ = r.emit("vector", "tensor_scalar",
                        dict(out=ta[:, 0:wd], in0=ta[:, 0:wd], scalar1=c_(CO_A), scalar2=1.0,
                             op0=ALU.subtract, op1=ALU.min), waits=[("sv", v_)])
            v_ = r.emit("vector", "tensor_scalar",
                        dict(out=oslot, in0=ta[:, 0:wd], scalar1=-1.0, scalar2=None,
                             op0=ALU.max),
                        waits=[("sv", v_), (ss, tn_store.get(i - 2, 0))])
            tn_store[i] = r.emit("sync", "dma_start",
                                 dict(out=dst, in_=oslot.rearrange("p (k c) -> p k c", c=128)),
                                 waits=[("sv", v_)], inc=(ss, 16))

        ti = 0
        for i in range(IC):
            tern_block(wgate_bc[i].rearrange("(k p) c -> p k c", p=128),
                       terng_s[i].rearrange("(k p) c -> p k c", p=128), 0, ti)
            ti += 1
        for i in range(IC):
            tern_block(wup_bc[i].rearrange("(k p) c -> p k c", p=128),
                       ternu_s[i].rearrange("(k p) c -> p k c", p=128), 1, ti)
            ti += 1
        for i in range(FC * 4):
            col, piece = i // 4, i % 4
            tern_block(wdown_bc[col][piece * (I // 4):(piece + 1) * (I // 4)]
                       .rearrange("(k p) c -> p k c", p=128),
                       ternd_s[col][piece * (I // 4):(piece + 1) * (I // 4)]
                       .rearrange("(k p) c -> p k c", p=128), 2, ti, width=I // 4)
            ti += 1
        barrier()

        SBG = fbuf[:, 0:G]
        SBU = fbuf[:, G:2 * G]
        SBQ = fbuf[:, 2 * G:3 * G]
        SBD = fbuf[:, 3 * G:4 * G]
        for s in range(NSUP):
            t0 = s * GT
            for tt_ in range(GT):
                sa1 = row_from_col(c_(CO_OS + t0 + tt_), 0)
                vb = bcast_row(0, SBG[:, tt_ * 128:(tt_ + 1) * 128])
                vb = r.emit("vector", "tensor_scalar",
                            dict(out=SBU[:, tt_ * 128:(tt_ + 1) * 128],
                                 in0=SBG[:, tt_ * 128:(tt_ + 1) * 128],
                                 scalar1=c_(CO_MW + 1), scalar2=None, op0=ALU.mult),
                            waits=[("sv", vb)])
                vb = r.emit("vector", "tensor_scalar",
                            dict(out=SBG[:, tt_ * 128:(tt_ + 1) * 128],
                                 in0=SBG[:, tt_ * 128:(tt_ + 1) * 128],
                                 scalar1=c_(CO_MW + 0), scalar2=None, op0=ALU.mult),
                            waits=[("sv", vb)])
            barrier()
            sp_mms = {}
            t8_of = {}
            g8_of = {}

            def gu_tail(jo):
                par = jo % 2
                PSG = PB[:, 0:G] if par == 0 else PA[:, 0:G]
                PSU = PB[:, 512:512 + G] if par == 0 else PA[:, 512:512 + G]
                PSH = PS[:, 0:G] if par == 0 else PA[:, 1024:1024 + G]
                v7 = r.emit("vector", "tensor_tensor",
                            dict(out=ta[:, 0:G], in0=PSG, in1=SBG, op=ALU.mult),
                            waits=[("sp", sp_mms[jo]), ("sv", g8_of.get(jo - 1, 0))])
                a7 = r.emit("scalar", "activation",
                            dict(out=ta[:, 0:G], in_=ta[:, 0:G], func=AF.Silu),
                            waits=[("sv", v7)])
                v8 = r.emit("vector", "tensor_tensor",
                            dict(out=ta[:, G:2 * G], in0=PSU, in1=SBU, op=ALU.mult),
                            waits=[("sv", v7)])
                g8 = r.emit("vector", "tensor_tensor",
                            dict(out=sb512[:, 0:G], in0=ta[:, 0:G], in1=ta[:, G:2 * G],
                                 op=ALU.mult),
                            waits=[("sv", v8), ("sa", a7),
                                   ("sp", t8_of.get(jo - 1, 0))])
                g8_of[jo] = g8
                t8 = r.emit("tensor", "matmul",
                            dict(out=PSH, lhsT=consts[:, 128:256], rhs=sb512[:, 0:G],
                                 start=True, stop=True, skip_group_check=True),
                            waits=[("sv", g8), ("sa", r.val("sa"))])
                t8_of[jo] = t8
                r.emit("scalar", "copy",
                       dict(out=A1[:, jo * G:(jo + 1) * G], in_=PSH),
                       waits=[("sp", t8)])

            for jo in range(IC):
                par = jo % 2
                qsem = "dq" if par == 0 else "dq2"
                wgd = r.emit("sync", "dma_start",
                             dict(out=wsb[:, par * 2 * D: par * 2 * D + D]
                                  .rearrange("p (k c) -> p k c", c=128),
                                  in_=terng_s[jo].rearrange("(k p) c -> p k c", p=128)),
                             waits=([("sp", sp_mms.get(jo - 2, 0))] if jo >= 2 else []),
                             inc=(qsem, 16))
                wud = r.emit("sync", "dma_start",
                             dict(out=wsb[:, par * 2 * D + D: par * 2 * D + 2 * D]
                                  .rearrange("p (k c) -> p k c", c=128),
                                  in_=ternu_s[jo].rearrange("(k p) c -> p k c", p=128)),
                             inc=(qsem, 16))
                qtot = r.val(qsem)
                PSG = PB[:, 0:G] if par == 0 else PA[:, 0:G]
                PSU = PB[:, 512:512 + G] if par == 0 else PA[:, 512:512 + G]
                for k in range(FC):
                    r.emit("tensor", "matmul",
                           dict(out=PSG,
                                lhsT=wsb[:, par * 2 * D + k * 128: par * 2 * D + (k + 1) * 128],
                                rhs=A2[:, k * T + t0 * 128: k * T + t0 * 128 + G],
                                start=(k == 0), stop=(k == FC - 1), skip_group_check=True),
                           waits=([(qsem, qtot), ("sv", r.val("sv")), ("sa", r.val("sa"))]
                                  if k == 0 else []),
                           inc=False)
                for k in range(FC):
                    r.emit("tensor", "matmul",
                           dict(out=PSU,
                                lhsT=wsb[:, par * 2 * D + D + k * 128:
                                         par * 2 * D + D + (k + 1) * 128],
                                rhs=A2[:, k * T + t0 * 128: k * T + t0 * 128 + G],
                                start=(k == 0), stop=(k == FC - 1), skip_group_check=True),
                           inc=(("sp", 1) if k == FC - 1 else False))
                sp_mms[jo] = r.val("sp")
                if jo >= 1:
                    gu_tail(jo - 1)
            gu_tail(IC - 1)
            barrier()
            for st in range(int(math.log2(IC))):
                step = 1 << st
                svS, sgS = r.val("sv"), r.val("sg")
                pi = 0
                for base in range(0, IC, 2 * step):
                    for off in range(step):
                        ia, ib = base + off, base + off + step
                        aslc = A1[:, ia * G:(ia + 1) * G]
                        bslc = A1[:, ib * G:(ib + 1) * G]
                        pi += 1
                        v9 = r.emit("vector", "tensor_tensor",
                                    dict(out=bslc, in0=aslc, in1=bslc, op=ALU.subtract),
                                    waits=[("sv", svS), ("sa", r.val("sa"))])
                        r.emit("vector", "scalar_tensor_tensor",
                               dict(out=aslc, in0=aslc, scalar=2.0, in1=bslc,
                                    op0=ALU.mult, op1=ALU.subtract),
                               waits=[("sv", v9)])
            barrier()
            svS, sgS = r.val("sv"), r.val("sg")
            vq = r.emit("vector", "tensor_tensor",
                        dict(out=ta[:, 0:G], in0=A1[:, 0:G], in1=A1[:, G:2 * G],
                             op=ALU.max), waits=[("sv", svS), ("sg", sgS)])
            gq = r.emit("vector", "tensor_tensor",
                        dict(out=xt[:, 0:G], in0=A1[:, 0:G], in1=A1[:, G:2 * G],
                             op=ALU.min),
                        waits=[("sv", svS), ("dst", r.val("dst"))])
            for jo in range(2, IC):
                vq = r.emit("vector", "tensor_tensor",
                            dict(out=ta[:, 0:G], in0=ta[:, 0:G],
                                 in1=A1[:, jo * G:(jo + 1) * G], op=ALU.max),
                            waits=[("sv", vq)])
                gq = r.emit("vector", "tensor_tensor",
                            dict(out=xt[:, 0:G], in0=xt[:, 0:G],
                                 in1=A1[:, jo * G:(jo + 1) * G], op=ALU.min),
                            waits=[("sv", gq)])
            vq = r.emit("vector", "tensor_scalar",
                        dict(out=xt[:, G:2 * G] if 2 * G <= D else xt[:, 0:G], in0=xt[:, 0:G],
                             scalar1=-1.0, scalar2=None, op0=ALU.mult),
                        waits=[("sv", gq), ("sv", vq)])
            vq = r.emit("vector", "tensor_tensor",
                        dict(out=ta[:, 0:G], in0=ta[:, 0:G],
                             in1=xt[:, G:2 * G] if 2 * G <= D else xt[:, 0:G],
                             op=ALU.max), waits=[("sv", vq)])
            vqb = r.emit("vector", "tensor_copy", dict(out=sb512[:, 0:G], in_=ta[:, 0:G]),
                         waits=[("sv", vq)])
            for tt_ in range(GT):
                tq1 = r.emit("tensor", "transpose",
                             dict(out=PT[:, 0:128], in_=sb512[:, tt_ * 128:(tt_ + 1) * 128],
                                  identity=consts[:, 0:128]),
                             waits=[("sv", vqb), ("sa", r.val("sa")), ("sv", r.val("sv"))])
                vq2 = r.emit("vector", "tensor_reduce",
                             dict(out=c_(CO_TMP), in_=PT[:, 0:128], axis=AX.X, op=ALU.max),
                             waits=[("sp", tq1)])
                vq2 = r.emit("vector", "tensor_scalar",
                             dict(out=c_(CO_TMP), in0=c_(CO_TMP), scalar1=float(FWC),
                                  scalar2=float(QEPS), op0=ALU.mult, op1=ALU.max),
                             waits=[("sv", vq2)])
                vq3 = r.emit("vector", "tensor_scalar",
                             dict(out=c_(CO_OSD + tt_), in0=c_(CO_TMP),
                                  scalar1=float(1.0 / 128.0), scalar2=c_(CO_MW + 2),
                                  op0=ALU.mult, op1=ALU.mult), waits=[("sv", vq2)])
                vq4 = r.emit("vector", "tensor_scalar",
                             dict(out=c_(CO_SCD + tt_), in0=c_(CO_TMP),
                                  scalar1=float(1.0 / (FWC * 128.0)), scalar2=None,
                                  op0=ALU.mult), waits=[("sv", vq3)])
                vq4 = r.emit("vector", "reciprocal",
                             dict(out=c_(CO_SCD + tt_), in_=c_(CO_SCD + tt_)),
                             waits=[("sv", vq4)])
                sa1 = row_from_col(c_(CO_SCD + tt_), 0)
                bcast_row(0, SBQ[:, tt_ * 128:(tt_ + 1) * 128])
                sa1 = row_from_col(c_(CO_OSD + tt_), 128)
                bcast_row(128, SBD[:, tt_ * 128:(tt_ + 1) * 128])
            svS, sgS = r.val("sv"), r.val("sg")
            for jo in range(IC):
                aslc = A1[:, jo * G:(jo + 1) * G]
                m5 = r.emit("vector", "tensor_tensor",
                            dict(out=ta[:, 0:G], in0=aslc, in1=SBQ, op=ALU.mult),
                            waits=[("sv", r.val("sv"))])
                wv_ = ("sv", m5)
                scr = ta[:, 0:G]
                vq5 = r.emit("vector", "tensor_scalar",
                             dict(out=scr, in0=scr, scalar1=c_(CO_A),
                                  scalar2=None, op0=ALU.add), waits=[wv_])
                vq5 = r.emit("vector", "tensor_scalar",
                             dict(out=scr, in0=scr, scalar1=c_(CO_A),
                                  scalar2=127.0, op0=ALU.subtract, op1=ALU.min),
                             waits=[("sv", vq5)])
                r.emit("vector", "tensor_scalar",
                       dict(out=aslc, in0=scr, scalar1=-128.0, scalar2=None,
                            op0=ALU.max), waits=[("sv", vq5)])
            barrier()
            ICH = IC // 2
            dn_mid = {}
            dn_end = {}
            st3 = {}
            sad_of = {}

            def dn_load(fo, half, waits):
                return r.emit("sync", "dma_start",
                              dict(out=wsb[:, half * 4096: half * 4096 + ICH * 128]
                                   .rearrange("p (k c) -> p k c", c=128),
                                   in_=ternd_s[fo][half * (I // 2):(half + 1) * (I // 2)]
                                   .rearrange("(k p) c -> p k c", p=128)),
                              waits=waits, inc=("dqa" if half == 0 else "dqb", 16))

            tp_of = {}

            def dn_tail(fo):
                par = fo % 2
                PSD = PA[:, 0:G] if par == 0 else PA[:, 512:512 + G]
                vd = r.emit("vector", "tensor_tensor",
                            dict(out=sb512[:, G:2 * G], in0=PSD, in1=SBD, op=ALU.mult),
                            waits=[("sp", max(dn_end[fo], tp_of.get(fo - 1, 0))),
                                   ("dq3", st3.get(fo - 1, 0))])
                last = None
                for tt_ in range(GT):
                    last = r.emit("tensor", "transpose",
                                  dict(out=PT[:, tt_ * 128:(tt_ + 1) * 128],
                                       in_=sb512[:, G + tt_ * 128: G + (tt_ + 1) * 128],
                                       identity=consts[:, 0:128]),
                                  waits=([("sv", vd), ("sa", r.val("sa"))] if tt_ == 0 else []),
                                  inc=(("sp", 1) if tt_ == GT - 1 else False))
                tp_of[fo] = r.val("sp")
                sad = r.emit("scalar", "copy",
                             dict(out=sb512[:, 0:G], in_=PT[:, 0:G]),
                             waits=[("sp", last), ("dq3", st3.get(fo - 1, 0))])
                sad_of[fo] = sad
                st3[fo] = r.emit("sync", "dma_start",
                                 dict(out=dfm_s[fo].rearrange("t p c -> p t c"),
                                      in_=sb512[:, 0:G].rearrange("p (t c) -> p t c", c=128)),
                                 waits=[("sa", sad)], inc=("dq3", 16))

            dn_load(0, 0, [("sv", r.val("sv")), ("sg", r.val("sg")), ("sa", r.val("sa"))])
            dn_load(0, 1, [])
            for fo in range(FC):
                par = fo % 2
                PSD = PA[:, 0:G] if par == 0 else PA[:, 512:512 + G]
                ta_tot, tb_tot = r.val("dqa"), r.val("dqb")
                for k in range(ICH):
                    r.emit("tensor", "matmul",
                           dict(out=PSD,
                                lhsT=wsb[:, (k % ICH) * 128:(k % ICH) * 128 + 128],
                                rhs=A1[:, k * G:(k + 1) * G],
                                start=(k == 0), stop=False, skip_group_check=True),
                           waits=([("dqa", ta_tot), ("sv", r.val("sv")),
                                   ("sg", r.val("sg")), ("sa", r.val("sa"))]
                                  if k == 0 else []),
                           inc=(("sp", 1) if k == ICH - 1 else False))
                dn_mid[fo] = r.val("sp")
                if fo + 1 < FC:
                    dn_load(fo + 1, 0, [("sp", dn_mid[fo])])
                for k in range(ICH, IC):
                    r.emit("tensor", "matmul",
                           dict(out=PSD,
                                lhsT=wsb[:, 4096 + (k % ICH) * 128: 4096 + (k % ICH) * 128 + 128],
                                rhs=A1[:, k * G:(k + 1) * G],
                                start=False, stop=(k == IC - 1), skip_group_check=True),
                           waits=([("dqb", tb_tot)] if k == ICH else []),
                           inc=(("sp", 1) if k == IC - 1 else False))
                dn_end[fo] = r.val("sp")
                if fo + 1 < FC:
                    dn_load(fo + 1, 1, [("sp", dn_end[fo])])
                if fo >= 1:
                    dn_tail(fo - 1)
            dn_tail(FC - 1)
            barrier()
            for tt_ in range(GT):
                t = t0 + tt_
                dg = r.emit("sync", "dma_start",
                            dict(out=wsb[:, 0:D].rearrange("p (k c) -> p k c", c=128),
                                 in_=dfm_s[:, tt_].rearrange("k p c -> p k c")),
                            waits=[("sv", r.val("sv")), ("sa", r.val("sa")),
                                   ("dq", r.val("dq")), ("dst", r.val("dst"))])
                sc = r.emit("scalar", "copy", dict(out=ta[:, :], in_=wsb[:, 0:D]),
                            waits=[("dq", dg)])
                r.emit("vector", waits=[("sa", sc)])
                vS = ln_stats(ta[:, :], D, scratch=xt[:, :])
                vf = r.emit("vector", "tensor_scalar",
                            dict(out=ta[:, :], in0=ta[:, :], scalar1=c_(CO_MEAN),
                                 scalar2=c_(CO_RSTD), op0=ALU.subtract, op1=ALU.mult),
                            waits=[("sv", vS)])
                df = r.emit("sync", "dma_start",
                            dict(out=xt[:, :], in_=x2_s[t * 128:(t + 1) * 128, :]),
                            waits=[("sv", r.val("sv")), ("dst", r.val("dst"))])
                vf = r.emit("vector", "tensor_tensor",
                            dict(out=xt[:, :], in0=ta[:, :], in1=xt[:, :], op=ALU.add),
                            waits=[("sv", vf), ("dq", df)])
                r.emit("sync", "dma_start",
                       dict(out=out[t * 128:(t + 1) * 128, :], in_=xt[:, :]),
                       waits=[("sv", vf)], inc=("dst", 16))
        r.emit("sync", waits=[("dst", r.val("dst"))])

        for ename in ENGINES:
            def mk(en):
                def f(eng):
                    r.replay(en, eng, sems)
                return f
            getattr(block, ename)(mk(ename))

    return nc


def _prep_inputs(L0_attn_x, L0_attn_wq, L0_attn_wk, L0_attn_wv, L0_attn_wg, L0_attn_wo,
                 L0_attn_ln_w, L0_attn_ln_b, L0_ffn_w_gate, L0_ffn_w_up, L0_ffn_w_down):
    D, I = 2048, 8192
    FC, IC = D // 128, I // 128

    def bc(wT, nblk):
        fi, fo = wT.shape
        return np.ascontiguousarray(wT.reshape(fi, nblk, 128).transpose(1, 0, 2))

    w = {}
    for nm, mat in (("wq", L0_attn_wq), ("wk", L0_attn_wk), ("wv", L0_attn_wv),
                    ("wg", L0_attn_wg), ("wo", L0_attn_wo)):
        w[nm + "_bc"] = bc(np.ascontiguousarray(np.asarray(mat, np.float32).T), FC)
    w["wgate_bc"] = bc(np.ascontiguousarray(np.asarray(L0_ffn_w_gate, np.float32).T), IC)
    w["wup_bc"] = bc(np.ascontiguousarray(np.asarray(L0_ffn_w_up, np.float32).T), IC)
    w["wdown_bc"] = bc(np.ascontiguousarray(np.asarray(L0_ffn_w_down, np.float32).T), FC)
    w["ln_w"] = np.asarray(L0_attn_ln_w, np.float32).reshape(1, D)
    w["ln_b"] = np.asarray(L0_attn_ln_b, np.float32).reshape(1, D)

    x = np.asarray(L0_attn_x, np.float32)
    in_maps = []
    for core in range(8):
        b, hh = core // 2, core % 2
        m = dict(w)
        m["x_own"] = np.ascontiguousarray(x[b, hh * 2048:(hh + 1) * 2048, :])
        m["x_pref"] = np.ascontiguousarray(x[b, 0:2048, :])
        m["maskin"] = np.array([[1.0 if hh == 1 else 0.0]], np.float32)
        in_maps.append(m)
    return in_maps


def kernel(**inputs):
    in_maps = _prep_inputs(**inputs)
    nc = build_program()
    res = run_bass_kernel_spmd(nc, in_maps, core_ids=list(range(8)))
    full = np.empty((4, 4096, 2048), np.float32)
    for core in range(8):
        b, hh = core // 2, core % 2
        full[b, hh * 2048:(hh + 1) * 2048, :] = np.asarray(res.results[core]["out"], np.float32)
    return full
